# version 08
# baseline (speedup 1.0000x reference)
import numpy as np
import ml_dtypes
import concourse.bass as bass
import concourse.mybir as mybir
from concourse.bass_utils import run_bass_kernel_spmd

F32 = mybir.dt.float32
BF16 = mybir.dt.bfloat16
U16 = mybir.dt.uint16
AF = mybir.ActivationFunctionType
OP = mybir.AluOpType

D = 4096
S = 2048
NB = 4
H = 32
HKV = 8
HD = 128
FF = 11008
NQ = 1024
KT = 32
T0 = 57.5
W0 = 12.5
NSUB = 5
NFULL = 10
WFULL0 = 3.2
W_SUB = [W0 * (0.5 ** (i + 1)) for i in range(NSUB)]
W_FULL = [WFULL0 * (0.5 ** (i + 1)) for i in range(NFULL)]
KS_OF_QT = [640, 768, 896, 1024, 1664, 1792, 1920, 2048]
SHIFT = 60.0
BIG = 1.0e38
INV_SQRT_HD = float(np.float32(1.0) / np.float32(np.sqrt(np.float32(HD))))
EPS = 1e-5
E_OF_QT = [2, 2, 2, 2, 4, 4, 4, 4]
MASK_K = 205

ENGINES = ["sync", "gpsimd", "scalar", "vector", "tensor"]

DEBUG_OUTS = []
PHASES = ["p0", "attn", "wo", "r1", "b1", "b2", "r2"]


class KB:

    def __init__(self, blk, sems):
        self.blk = blk
        self.sems = sems
        self.cur = None
        self.E = None
        self.c = {}

    def run(self, script):
        for name in ENGINES:
            self.cur = name
            for kk in self.sems:
                self.c[kk] = 0

            def body(e):
                self.E = e
                script(self)

            getattr(self.blk, name)(body)

    def do(self, eng, fn, inc=None, n=1):
        if n is None:
            inc = None
        if self.cur == eng:
            inst = fn(self.E)
            if inc is not None:
                inst.then_inc(self.sems[inc], n)
        if inc is not None:
            self.c[inc] += n
            return self.c[inc]
        return None

    def wait(self, eng, ctr, val):
        if val is None or val <= 0:
            return
        if self.cur == eng:
            self.E.wait_ge(self.sems[ctr], val)

    def nop(self, eng):
        if self.cur == eng:
            self.E.engine_nop()

    def barrier(self):
        snap = dict(self.c)
        for eng in ENGINES:
            for ctr, val in snap.items():
                self.wait(eng, ctr, val)


class Arena:
    def __init__(self, ap, words):
        self.ap = ap
        self.words = words
        self.off = 0
        self.marks = []

    def f32(self, w):
        assert self.off + w <= self.words, f"arena overflow {self.off}+{w}"
        r = self.ap[:, self.off:self.off + w]
        self.off += w
        return r

    def bf16(self, n):
        assert n % 2 == 0
        return self.f32(n // 2).bitcast(BF16)

    def u16(self, n):
        assert n % 2 == 0
        return self.f32(n // 2).bitcast(U16)

    def push(self):
        self.marks.append(self.off)

    def pop(self):
        self.off = self.marks.pop()


def build(nc):
    dt = nc.dram_tensor
    xq = dt("xq", [NQ, D], F32, kind="ExternalInput").ap()
    xkv = dt("xkv", [S, D], F32, kind="ExternalInput").ap()
    wqt = dt("wqt", [D, D], F32, kind="ExternalInput").ap()
    wkt = dt("wkt", [D, HKV * HD], F32, kind="ExternalInput").ap()
    wvt = dt("wvt", [D, HKV * HD], F32, kind="ExternalInput").ap()
    wot = dt("wot", [D, D], F32, kind="ExternalInput").ap()
    wgt = dt("wgt", [D, FF], F32, kind="ExternalInput").ap()
    wut = dt("wut", [D, FF], F32, kind="ExternalInput").ap()
    wdt = dt("wdt", [FF, D], F32, kind="ExternalInput").ap()
    cosq = dt("cosq", [128, 8, HD], F32, kind="ExternalInput").ap()
    sinq = dt("sinq", [128, 8, HD], F32, kind="ExternalInput").ap()
    cosk = dt("cosk", [128, 16, HD], F32, kind="ExternalInput").ap()
    sink = dt("sink", [128, 16, HD], F32, kind="ExternalInput").ap()
    consts = dt("consts", [128, 104], F32, kind="ExternalInput").ap()
    iota_in = dt("iota_in", [128, S], U16, kind="ExternalInput").ap()
    ident_in = dt("ident_in", [128, 128], BF16, kind="ExternalInput").ap()

    qt_scr = dt("qt_scr", [H, 128, NQ], BF16, kind="Internal").ap()
    v_scr = dt("v_scr", [16, 128, HKV * HD], BF16, kind="Internal").ap()
    ot_scr = dt("ot_scr", [8, 128, D], BF16, kind="Internal").ap()
    attn_scr = dt("attn_scr", [NQ, D], F32, kind="Internal").ap()
    x2_scr = dt("x2_scr", [NQ, D], F32, kind="Internal").ap()
    x2t_scr = dt("x2t_scr", [KT, 128, NQ], BF16, kind="Internal").ap()
    ht_scr = dt("ht_scr", [FF // 128, 128, NQ], BF16, kind="Internal").ap()
    y_scr = dt("y_scr", [NQ, D], F32, kind="Internal").ap()
    out_d = dt("out", [NQ, D], F32, kind="ExternalOutput").ap()

    dbg = {}
    for nm, shape, dty in [
        ("dbg_ktr", [128, HKV * S], BF16), ("dbg_vtr", [128, 16 * HKV * HD], BF16),
        ("dbg_qt", [H, 128, NQ], BF16), ("dbg_shat", [128, 4 * 2048], F32),
        ("dbg_state", [128, 64 * 16], F32), ("dbg_ot", [8, 128, D], BF16),
        ("dbg_attn", [NQ, D], F32), ("dbg_x2", [NQ, D], F32),
        ("dbg_ht", [FF // 128, 128, NQ], BF16), ("dbg_y", [NQ, D], F32),
        ("dbg_x2t", [KT, 128, NQ], BF16),
    ]:
        if nm in DEBUG_OUTS:
            dbg[nm] = dt(nm, shape, dty, kind="ExternalOutput").ap()

    AW = 51500
    SEM_NAMES = ["sp", "sa", "sv", "init", "cs", "xa", "xb", "qtb",
                 "wva", "wvb", "wua", "wub", "qev", "otd", "otba", "otbb",
                 "oeva", "oevb", "x2w", "x2tw", "htw", "hra", "hrb", "yw",
                 "outw", "dbgw"]
    from contextlib import ExitStack as _ES
    _es_sem = _ES()
    with (
        nc.sbuf_tensor("arena", [128, AW], F32) as arena_t,
        nc.psum_tensor("pA", [128, 2048], F32) as pA,
        nc.psum_tensor("pB", [128, 2048], F32) as pB,
        _es_sem,
    ):
        sems = {nm: _es_sem.enter_context(nc.semaphore(nm)) for nm in SEM_NAMES}
        blk = _es_sem.enter_context(nc.Block())
        ar = Arena(arena_t[:], AW)
        g_ident = ar.bf16(128)
        g_iota = ar.u16(2048)
        g_cst = ar.f32(104)
        g_stats = ar.f32(64)
        g_st = ar.f32(17 * 64)
        g_xin = ar.f32(2 * D)
        g_junkb = ar.bf16(2048)
        g_junkc = g_junkb
        g_lnout = ar.bf16(4096)

        def st(i):
            return g_st[:, i * 64:(i + 1) * 64]
        (ST_TC, ST_NG, ST_CNT, ST_P, ST_SU, ST_THR, ST_D, ST_TS, ST_NTS,
         ST_CF, ST_HALF, ST_DREP, ST_MV, ST_T4, ST_SUMP, ST_RCP,
         ST_THRQ) = range(17)

        ar.push()
        a_ktr = ar.bf16(HKV * S)
        ar.push()
        p_ln1t = ar.bf16(KT * 512)
        p_wbuf = [ar.bf16(8 * 512), ar.bf16(8 * 512)]
        p_cost = ar.f32(4 * HD)
        p_sint = ar.f32(4 * HD)
        p_ktmp = ar.f32(512)
        p_rt1 = ar.f32(512)
        p_rt2 = ar.f32(512)
        p_krot = ar.bf16(512)
        p_qev = ar.bf16(512)
        ar.pop()
        ar.push()
        t_shat = [[ar.f32(2048) for _ in range(4)] for _ in range(2)]
        t_qtb = ar.bf16(H * 128)
        t_j60 = ar.bf16(2048)
        t_pb = [ar.bf16(2048), ar.bf16(2048)]
        t_eb = [ar.bf16(2048), ar.bf16(2048)]
        t_pts = ar.bf16(16 * 128)
        t_ot = ar.bf16(H * 128)
        t_m = g_xin[:, 0:2048]
        t_vg = [ar.bf16(2048), ar.bf16(2048)]
        ar.pop()
        ar.pop()

        ar.push()
        w_wot = [ar.bf16(KT * 512), ar.bf16(KT * 512)]
        w_otb = [ar.bf16(D), ar.bf16(D)]
        w_oev = [ar.f32(512), ar.f32(512)]
        ar.pop()
        ar.push()
        g_scr1 = ar.f32(D)
        g_scr2 = ar.f32(D)
        r_x2tb = ar.bf16(4 * 128)
        ar.pop()
        ar.push()
        b1_wg = [ar.bf16(KT * 256), ar.bf16(KT * 256)]
        b1_wu = [ar.bf16(KT * 256), ar.bf16(KT * 256)]
        b1_x2t = ar.bf16(KT * NQ)
        b1_gs = [ar.bf16(256), ar.bf16(256)]
        b1_hb = [ar.bf16(256), ar.bf16(256)]
        b1_htb = [ar.bf16(256), ar.bf16(256)]
        ar.pop()
        ar.push()
        b2_wd = [ar.bf16(8 * 512), ar.bf16(8 * 512)]
        b2_ht = [ar.bf16(8 * NQ), ar.bf16(8 * NQ)]
        b2_yev = ar.f32(D)
        ar.pop()

        def script(k):
            k.do("sync", lambda e: e.dma_start(out=g_ident, in_=ident_in), "init", 16)
            k.do("sync", lambda e: e.dma_start(out=g_iota, in_=iota_in), "init", 16)
            init_d = k.do("sync", lambda e: e.dma_start(out=g_cst, in_=consts), "init", 16)
            HB = ST_HALF * 64
            k.do("vector", lambda e: e.memset(g_st[:, 0:HB], 0.0))
            k.do("vector", lambda e: e.memset(st(ST_HALF)[:, 0:2], 0.5))
            k.do("vector", lambda e: e.memset(st(ST_HALF)[:, 2:4], 1.0))
            k.do("vector", lambda e: e.memset(st(ST_HALF)[:, 4:64], 0.0))
            k.do("vector", lambda e: e.memset(g_st[:, HB + 64:], 0.0), "sv", 1)
            for eng in ("scalar", "vector", "tensor", "gpsimd"):
                k.wait(eng, "init", init_d)

            def afence():
                return k.do("scalar", lambda e: e.activation(
                    g_junkb[:, 0:64], g_junkb[:, 0:64], AF.Identity,
                    bias=0.0, scale=1.0), "sa", 1)

            def vfence():
                return k.do("vector", lambda e: e.tensor_copy(
                    g_junkc[:, 64:128], g_junkc[:, 0:64]), "sv", 1)

            def ln_stats(xv, wait_ctr=None, wait_val=None, pe_war=0):
                k.wait("scalar", "sp", pe_war)
                if wait_ctr is not None:
                    k.wait("scalar", wait_ctr, wait_val)
                k.do("scalar", lambda e: e.activation(
                    g_lnout[:], xv, AF.Identity, bias=0.0, scale=1.0,
                    accum_out=g_stats[:, 0:1]))
                k.do("scalar", lambda e: e.activation(
                    g_lnout[:], xv, AF.Square, bias=0.0, scale=1.0,
                    accum_out=g_stats[:, 1:2]))
                a_s = afence()
                k.wait("vector", "sa", a_s)
                k.do("vector", lambda e: e.tensor_scalar(
                    g_stats[:, 2:3], g_stats[:, 0:1], 1.0 / D, None, op0=OP.mult))
                k.do("vector", lambda e: e.tensor_scalar(
                    g_stats[:, 3:4], g_stats[:, 1:2], 1.0 / D, EPS,
                    op0=OP.mult, op1=OP.add))
                k.nop("vector")
                k.do("vector", lambda e: e.tensor_tensor(
                    g_stats[:, 4:5], g_stats[:, 2:3], g_stats[:, 2:3], op=OP.mult))
                k.nop("vector")
                v_v = k.do("vector", lambda e: e.tensor_tensor(
                    g_stats[:, 5:6], g_stats[:, 3:4], g_stats[:, 4:5],
                    op=OP.subtract), "sv", 1)
                k.wait("scalar", "sv", v_v)
                k.do("scalar", lambda e: e.activation(
                    g_stats[:, 6:7], g_stats[:, 5:6], AF.Ln, bias=0.0, scale=1.0))
                k.do("scalar", lambda e: e.activation(
                    g_stats[:, 7:8], g_stats[:, 6:7], AF.Exp, bias=0.0, scale=-0.5))
                return afence()

            ktr_done = 0
            vtr_done = 0
            qscr_done = 0
            if "p0" in PHASES:
                tr_evac = [0, 0]
                ln1t_pe = 0
                wave_pe = [0, 0]
                proj_evac = [0, 0, 0, 0]
                ktmp_v = 0
                krot_pe = 0
                kq_evac = 0
                qev_dma = 0
                groups = [("kv", g) for g in range(4)] + [("q", g) for g in range(2)]
                for (kind, g) in groups:
                    src = xkv if kind == "kv" else xq
                    cosg = cosk if kind == "kv" else cosq
                    sing = sink if kind == "kv" else sinq
                    gt0 = g * 4
                    k.wait("sync", "sv", ktmp_v)
                    k.do("sync", lambda e, gt0=gt0, cosg=cosg: e.dma_start(
                        out=p_cost.rearrange("p (t d) -> p t d", t=4),
                        in_=cosg[:, gt0:gt0 + 4, :]), "cs", 16)
                    cs_d = k.do("sync", lambda e, gt0=gt0, sing=sing: e.dma_start(
                        out=p_sint.rearrange("p (t d) -> p t d", t=4),
                        in_=sing[:, gt0:gt0 + 4, :]), "cs", 16)
                    ln_pe_hist = {}
                    for tt in range(4):
                        row0 = (gt0 + tt) * 128
                        xv = g_xin[:, (tt % 2) * D:(tt % 2 + 1) * D]
                        xsem = "xa" if tt % 2 == 0 else "xb"
                        if tt >= 2:
                            k.wait("sync", "sv", ln_v_hist[tt - 2])
                        dma_d = k.do("sync", lambda e, xv=xv, row0=row0, src=src:
                                     e.dma_start(out=xv, in_=src[row0:row0 + 128, :]),
                                     xsem, 16)
                        a_r = ln_stats(xv, xsem, dma_d, pe_war=ln1t_pe)
                        k.wait("vector", "sa", a_r)
                        k.wait("vector", "sp", ln1t_pe)
                        v_n = k.do("vector", lambda e, xv=xv: e.tensor_scalar(
                            g_lnout[:], xv, g_stats[:, 2:3], g_stats[:, 7:8],
                            op0=OP.subtract, op1=OP.mult), "sv", 1)
                        if tt == 0:
                            ln_v_hist = {}
                        ln_v_hist[tt] = v_n
                        for e8 in range(8):
                            bank = pB[:, 0:512] if e8 % 2 == 0 else pB[:, 512:1024]
                            k.wait("tensor", "sv", v_n)
                            k.wait("tensor", "sa", tr_evac[e8 % 2])
                            for c4 in range(4):
                                c = e8 * 4 + c4
                                k.do("tensor", lambda e, bank=bank, c4=c4, c=c:
                                     e.matmul(bank[:, c4 * 128:(c4 + 1) * 128],
                                              g_lnout[:, c * 128:(c + 1) * 128],
                                              g_ident, start=True, stop=True),
                                     "sp", 1 if c4 == 3 else None)
                            ln1t_pe = k.c["sp"]
                            k.wait("scalar", "sp", ln1t_pe)
                            dst = p_ln1t.rearrange("p (c t) -> p c t", c=KT)[
                                :, e8 * 4:e8 * 4 + 4, tt * 128:(tt + 1) * 128]
                            tr_evac[e8 % 2] = k.do(
                                "scalar", lambda e, bank=bank, dst=dst: e.activation(
                                    dst, bank.rearrange("p (c t) -> p c t", c=4),
                                    AF.Identity, bias=0.0, scale=1.0), "sa", 1)
                    ln_done_a = k.c["sa"]
                    projs = ([("k", wkt, 2), ("v", wvt, 2)] if kind == "kv"
                             else [("q", wqt, 8)])
                    for (pname, wsrc, nocs) in projs:
                        for oc in range(nocs):
                            def wave_dma(w):
                                wb = p_wbuf[w % 2]
                                k.wait("gpsimd", "sp", wave_pe[w % 2])
                                return k.do(
                                    "gpsimd", lambda e, w=w, wb=wb, wsrc=wsrc, oc=oc:
                                    e.dma_start(
                                        out=wb.rearrange("p (c o) -> p c o", c=8),
                                        in_=wsrc[w * 1024:(w + 1) * 1024,
                                                 oc * 512:(oc + 1) * 512].rearrange(
                                            "(c p) o -> p c o", p=128)),
                                    "wva" if w % 2 == 0 else "wvb", 16)

                            def wave_mm(w, wd):
                                wb = p_wbuf[w % 2]
                                k.wait("tensor", "wva" if w % 2 == 0 else "wvb", wd)
                                for tt in range(4):
                                    for c8 in range(8):
                                        c = w * 8 + c8
                                        last = (w == 3 and tt == 3 and c8 == 7)
                                        k.do("tensor",
                                             lambda e, tt=tt, c=c, c8=c8, wb=wb:
                                             e.matmul(
                                                 pA[:, tt * 512:(tt + 1) * 512],
                                                 p_ln1t[:, c * 512 + tt * 128:
                                                        c * 512 + (tt + 1) * 128],
                                                 wb[:, c8 * 512:(c8 + 1) * 512],
                                                 start=(c == 0), stop=(c == 31)),
                                             "sp", 1 if (tt == 3 and c8 == 7) else None)
                                wave_pe[w % 2] = k.c["sp"]

                            wd0 = wave_dma(0)
                            wd1 = wave_dma(1)
                            k.wait("tensor", "sa", ln_done_a)
                            k.wait("tensor", "sa", max(proj_evac))
                            wave_mm(0, wd0)
                            wd2 = wave_dma(2)
                            wave_mm(1, wd1)
                            wd3 = wave_dma(3)
                            wave_mm(2, wd2)
                            wave_mm(3, wd3)
                            mm_d = k.c["sp"]
                            for tt in range(4):
                                bank = pA[:, tt * 512:(tt + 1) * 512]
                                kt_i = gt0 + tt
                                if pname == "v":
                                    k.wait("scalar", "sp", mm_d)
                                    k.wait("scalar", "qev", qev_dma)
                                    a_v = k.do("scalar",
                                               lambda e, bank=bank: e.activation(
                                        p_qev, bank, AF.Identity, bias=0.0,
                                        scale=1.0), "sa", 1)
                                    proj_evac[tt] = a_v
                                    k.wait("sync", "sa", a_v)
                                    qev_dma = k.do("sync",
                                                   lambda e, kt_i=kt_i, oc=oc:
                                                   e.dma_start(
                                        out=v_scr[kt_i, :, oc * 512:(oc + 1) * 512],
                                        in_=p_qev), "qev", 16)
                                    vtr_done = a_v
                                    continue
                                k.wait("scalar", "sp", mm_d)
                                k.wait("scalar", "sv", ktmp_v)
                                a_e = k.do("scalar", lambda e, bank=bank: e.activation(
                                    p_ktmp, bank, AF.Identity, bias=0.0, scale=1.0),
                                    "sa", 1)
                                proj_evac[tt] = a_e
                                k.wait("vector", "sa", a_e)
                                k.wait("vector", "cs", cs_d)
                                k.wait("vector", "sp", krot_pe)
                                c4v = p_cost.rearrange("p (t d) -> p t d", t=4)[
                                    :, tt:tt + 1, :].to_broadcast([128, 4, HD])
                                k.do("vector", lambda e, c4v=c4v: e.tensor_tensor(
                                    p_rt1.rearrange("p (h d) -> p h d", h=4),
                                    p_ktmp.rearrange("p (h d) -> p h d", h=4),
                                    c4v, op=OP.mult))
                                s8 = p_sint.rearrange("p (t two d) -> p t two d",
                                                      t=4, two=2)
                                slo = s8[:, tt:tt + 1, 0:1, :].rearrange(
                                    "p a b d -> p (a b) d").to_broadcast([128, 4, 64])
                                shi = s8[:, tt:tt + 1, 1:2, :].rearrange(
                                    "p a b d -> p (a b) d").to_broadcast([128, 4, 64])
                                k8 = p_ktmp.rearrange("p (h two d) -> p h two d",
                                                      h=4, two=2)
                                t28 = p_rt2.rearrange("p (h two d) -> p h two d",
                                                      h=4, two=2)
                                k.do("vector", lambda e, k8=k8, t28=t28, slo=slo:
                                     e.tensor_tensor(t28[:, :, 0, :], k8[:, :, 1, :],
                                                     slo, op=OP.mult))
                                k.do("vector", lambda e, k8=k8, t28=t28, shi=shi:
                                     e.tensor_tensor(t28[:, :, 1, :], k8[:, :, 0, :],
                                                     shi, op=OP.mult))
                                ktmp_v = k.do("vector", lambda e: e.tensor_tensor(
                                    p_krot, p_rt1, p_rt2, op=OP.add), "sv", 1)
                                k.wait("tensor", "sv", ktmp_v)
                                k.wait("tensor", "sa", kq_evac)
                                for hh in range(4):
                                    k.do("tensor", lambda e, hh=hh: e.matmul(
                                        pB[:, 1024 + hh * 128:1024 + (hh + 1) * 128],
                                        p_krot[:, hh * 128:(hh + 1) * 128],
                                        g_ident, start=True, stop=True),
                                        "sp", 1 if hh == 3 else None)
                                krot_pe = k.c["sp"]
                                k.wait("scalar", "sp", krot_pe)
                                if pname == "k":
                                    dstk = a_ktr.rearrange("p (h t) -> p h t", h=HKV)[
                                        :, 4 * oc:4 * oc + 4,
                                        kt_i * 128:(kt_i + 1) * 128]
                                    kq_evac = k.do("scalar",
                                                   lambda e, dstk=dstk: e.activation(
                                        dstk,
                                        pB[:, 1024:1536].rearrange("p (h t) -> p h t", h=4),
                                        AF.Identity,
                                        bias=0.0, scale=1.0), "sa", 1)
                                    ktr_done = kq_evac
                                else:
                                    k.wait("scalar", "qev", qev_dma)
                                    kq_evac = k.do("scalar", lambda e: e.activation(
                                        p_qev, pB[:, 1024:1536], AF.Identity,
                                        bias=0.0, scale=1.0), "sa", 1)
                                    k.wait("sync", "sa", kq_evac)
                                    ttq = gt0 + tt
                                    for hh in range(4):
                                        qev_dma = k.do(
                                            "sync", lambda e, hh=hh, oc=oc, ttq=ttq:
                                            e.dma_start(
                                                out=qt_scr[4 * oc + hh, :,
                                                           ttq * 128:(ttq + 1) * 128],
                                                in_=p_qev[:, hh * 128:(hh + 1) * 128]),
                                            "qev", 16)
                                    qscr_done = qev_dma
                if "dbg_ktr" in dbg:
                    k.barrier()
                    k.do("sync", lambda e: e.dma_start(
                        out=dbg["dbg_ktr"], in_=a_ktr), "dbgw", 16)

                if "dbg_qt" in dbg:
                    k.barrier()
                    for hh in range(H):
                        k.do("sync", lambda e, hh=hh: e.dma_start(
                            out=dbg["dbg_qt"][hh], in_=qt_scr[hh, :, :]), "dbgw", 16)

            k.barrier()

            if "attn" in PHASES:
                pa_free = [0, 0]
                shat_act = [[0, 0, 0, 0], [0, 0, 0, 0]]
                pts_bank_evac = [0, 0]
                pts_war_pe = 0
                av_evac = 0
                ot_dma = 0
                last_s_mm = 0
                stt_done = {-2: 0, -1: 0}
                pnorm_done = {-2: 0, -1: 0}
                tr_pe_par = [0, 0]
                vg_dma = [0, 0]
                vg_pe = [0, 0]
                qt_setup_done = {}
                evac_ms = {}
                GL = [(qt, hg) for qt in range(8) for hg in range(8)]

                def stageA(Gi):
                    global_vars = None
                    qt, hg = GL[Gi]
                    s = Gi % 2
                    E = E_OF_QT[qt]
                    KS = KS_OF_QT[qt]
                    KT4 = KS // 128
                    nonlocal last_s_mm, qd_ms
                    if hg == 0:
                        k.do("vector", lambda e, KS=KS, qt=qt: e.tensor_scalar(
                            t_j60[:, 0:KS], g_iota[:, 0:KS], g_cst[:, qt:qt + 1],
                            BIG, op0=OP.is_ge, op1=OP.mult))
                        k.do("vector", lambda e, KS=KS: e.tensor_scalar(
                            t_j60[:, 0:KS], t_j60[:, 0:KS], SHIFT, None,
                            op0=OP.subtract))
                        vfence()
                        k.wait("sync", "qev", qscr_done)
                        k.wait("sync", "sp", last_s_mm)
                        qd_ms = k.do("sync", lambda e, qt=qt: e.dma_start(
                            out=t_qtb.rearrange("p (h t) -> p h t", h=H),
                            in_=qt_scr[:, :, qt * 128:(qt + 1) * 128].rearrange(
                                "h p t -> p h t")), "qtb", 16)
                    vj = k.c["sv"]
                    gkv = hg
                    vsem = "hra" if Gi % 2 == 0 else "hrb"
                    k.wait("sync", "sp", vg_pe[Gi % 2])
                    k.wait("sync", "qev", qscr_done)
                    vg_dma[Gi % 2] = k.do("sync",
                                          lambda e, KT4=KT4, gkv=gkv, sl=Gi % 2:
                                          e.dma_start(
                        out=t_vg[sl][:, 0:KT4 * 128].rearrange(
                            "p (c d) -> p c d", c=KT4),
                        in_=v_scr[0:KT4, :, gkv * 128:(gkv + 1) * 128].rearrange(
                            "c p d -> p c d")), vsem, 16)
                    ev = []
                    for j in range(4):
                        h = 4 * hg + j
                        slot = (j % 2) if E == 2 else 0
                        sb = pA[:, slot * 1024: slot * 1024 + KS]
                        k.wait("tensor", "qtb", qd_ms)
                        k.wait("tensor", "sa", ktr_done)
                        k.wait("tensor", "sv", pa_free[slot])
                        k.wait("tensor", "sa", shat_act[s][j])
                        for c in range(E):
                            k.do("tensor", lambda e, slot=slot, c=c, h=h, gkv=gkv:
                                 e.matmul(
                                pA[:, slot * 1024 + c * 512:
                                   slot * 1024 + (c + 1) * 512],
                                t_qtb[:, h * 128:(h + 1) * 128],
                                a_ktr[:, gkv * S + c * 512:
                                      gkv * S + (c + 1) * 512],
                                start=True, stop=True),
                                "sp", 1 if c == E - 1 else None)
                        pe_sm = k.c["sp"]
                        last_s_mm = pe_sm
                        k.wait("vector", "sp", pe_sm)
                        k.wait("vector", "sa", shat_act[s][j])
                        pa_free[slot] = k.do(
                            "vector", lambda e, s=s, j=j, sb=sb, KS=KS:
                            e.scalar_tensor_tensor(
                                t_shat[s][j][:, 0:KS], sb, INV_SQRT_HD,
                                t_j60[:, 0:KS], op0=OP.mult, op1=OP.subtract),
                            "sv", 1)
                        ev.append(k.c["sv"])
                    evac_ms[Gi] = ev

                def stageBC(Gi):
                    nonlocal ot_dma, pts_war_pe, av_evac
                    qt, hg = GL[Gi]
                    s = Gi % 2
                    E = E_OF_QT[qt]
                    KS = KS_OF_QT[qt]
                    KT4 = KS // 128
                    gkv = hg
                    sh = t_shat[s]
                    if hg == 0:
                        k.do("vector", lambda e, qt=qt: e.tensor_copy(
                            st(ST_THR)[:, 0:2],
                            g_cst[:, 16 + qt:17 + qt].to_broadcast([128, 2])))
                        k.do("vector", lambda e, qt=qt: e.tensor_copy(
                            st(ST_THR)[:, 2:4],
                            g_cst[:, 8 + qt:9 + qt].to_broadcast([128, 2])))
                        k.do("vector", lambda e, qt=qt: e.tensor_copy(
                            st(ST_THRQ)[:, 0:2],
                            g_cst[:, 96 + qt:97 + qt].to_broadcast([128, 2])))
                        k.do("vector", lambda e, qt=qt: e.tensor_copy(
                            st(ST_THRQ)[:, 2:4],
                            g_cst[:, 88 + qt:89 + qt].to_broadcast([128, 2])))
                        k.nop("vector")
                    k.do("vector", lambda e: e.tensor_scalar(
                        st(ST_TC)[:], st(ST_HALF)[:], 0.0, T0,
                        op0=OP.mult, op1=OP.add))
                    k.do("vector", lambda e: e.tensor_scalar(
                        st(ST_NG)[:], st(ST_HALF)[:], 0.0, -T0,
                        op0=OP.mult, op1=OP.add))
                    vng = vfence()
                    ev = evac_ms[Gi]
                    del evac_ms[Gi]
                    for it in range(NSUB + NFULL):
                        subp = it < NSUB
                        w_i = W_SUB[it] if subp else W_FULL[it - NSUB]
                        thr_blk = ST_THRQ if subp else ST_THR
                        k.wait("scalar", "sv", vng)
                        k.wait("scalar", "sv", ev[1])
                        for j in (0, 1):
                            if subp:
                                k.do("scalar", lambda e, j=j, KS=KS, sh=sh:
                                     e.activation(
                                    g_junkb[:, 0:KS // 4], sh[j][:, 0:KS:4],
                                    AF.Sign, bias=st(ST_NG)[:, j:j + 1],
                                    scale=1.0,
                                    accum_out=st(ST_CNT)[:, j:j + 1]))
                            else:
                                k.do("scalar", lambda e, j=j, KS=KS, sh=sh:
                                     e.activation(
                                    g_junkb[:, 0:KS], sh[j][:, 0:KS],
                                    AF.Sign, bias=st(ST_NG)[:, j:j + 1],
                                    scale=1.0,
                                    accum_out=st(ST_CNT)[:, j:j + 1]))
                        act_c = afence()
                        k.wait("vector", "sv", ev[3])
                        for j in (2, 3):
                            if subp:
                                k.do("vector", lambda e, j=j, KS=KS, sh=sh:
                                     e.tensor_scalar(
                                    g_junkc[:, 0:KS // 4], sh[j][:, 0:KS:4],
                                    st(ST_TC)[:, j:j + 1], None, op0=OP.is_ge,
                                    op1=OP.add,
                                    accum_out=st(ST_CNT)[:, j:j + 1]))
                            else:
                                k.do("vector", lambda e, j=j, KS=KS, sh=sh:
                                     e.tensor_scalar(
                                    g_junkc[:, 0:KS], sh[j][:, 0:KS],
                                    st(ST_TC)[:, j:j + 1], None, op0=OP.is_ge,
                                    op1=OP.add,
                                    accum_out=st(ST_CNT)[:, j:j + 1]))
                        k.wait("vector", "sa", act_c)
                        k.nop("vector")
                        vfence()
                        k.do("vector", lambda e, thr_blk=thr_blk: e.tensor_tensor(
                            st(ST_P)[:], st(ST_CNT)[:], st(thr_blk)[:],
                            op=OP.is_ge))
                        k.do("vector", lambda e, w_i=w_i: e.tensor_scalar(
                            st(ST_SU)[:], st(ST_P)[:], 2.0 * w_i, -w_i,
                            op0=OP.mult, op1=OP.add))
                        k.do("vector", lambda e: e.tensor_tensor(
                            st(ST_TC)[:], st(ST_TC)[:], st(ST_SU)[:], op=OP.add))
                        k.do("vector", lambda e: e.tensor_scalar(
                            st(ST_NG)[:], st(ST_TC)[:], -1.0, None, op0=OP.mult))
                        vng = vfence()
                    w_last = W_FULL[-1]
                    k.do("vector", lambda e, w_last=w_last: e.tensor_scalar(
                        st(ST_TS)[:], st(ST_TC)[:], 2.0 * w_last, None,
                        op0=OP.add))
                    k.do("vector", lambda e: e.tensor_scalar(
                        st(ST_NG)[:], st(ST_TS)[:], -1.0, None, op0=OP.mult))
                    vng = vfence()
                    k.wait("scalar", "sv", vng)
                    for j in (0, 1):
                        k.do("scalar", lambda e, j=j, KS=KS, sh=sh: e.activation(
                            g_junkb[:, 0:KS], sh[j][:, 0:KS], AF.Sign,
                            bias=st(ST_NG)[:, j:j + 1], scale=1.0,
                            accum_out=st(ST_CF)[:, j:j + 1]))
                    act_cf = afence()
                    for j in (2, 3):
                        k.do("vector", lambda e, j=j, KS=KS, sh=sh: e.tensor_scalar(
                            g_junkc[:, 0:KS], sh[j][:, 0:KS],
                            st(ST_TS)[:, j:j + 1], None, op0=OP.is_ge,
                            op1=OP.add, accum_out=st(ST_CF)[:, j:j + 1]))
                    k.wait("vector", "sa", act_cf)
                    k.nop("vector")
                    vfence()
                    k.do("vector", lambda e: e.tensor_tensor(
                        st(ST_D)[:], st(ST_THR)[:], st(ST_CF)[:], op=OP.subtract))
                    k.do("vector", lambda e: e.tensor_tensor(
                        st(ST_D)[:], st(ST_D)[:], st(ST_HALF)[:], op=OP.mult))
                    for j in range(4):
                        k.do("vector", lambda e, j=j, KS=KS, sh=sh:
                             e.scalar_tensor_tensor(
                            t_m[:, 0:KS], sh[j][:, 0:KS],
                            st(ST_TS)[:, j:j + 1], sh[j][:, 0:KS],
                            op0=OP.is_lt, op1=OP.mult))
                        k.do("vector", lambda e, j=j, KS=KS: e.max(
                            out=st(ST_MV)[:, j * 8:(j + 1) * 8],
                            in_=t_m[:, 0:KS]))
                    for j in range(4):
                        k.do("vector", lambda e, j=j: e.tensor_copy(
                            st(ST_DREP)[:, j * 8:(j + 1) * 8],
                            st(ST_D)[:, j:j + 1].to_broadcast([128, 8])))
                    k.nop("vector")
                    k.do("vector", lambda e: e.tensor_tensor(
                        st(ST_P)[:], g_cst[:, 24:88], st(ST_DREP)[:],
                        op=OP.is_gt))
                    k.do("vector", lambda e: e.tensor_scalar(
                        st(ST_SU)[:], st(ST_P)[:], BIG, None, op0=OP.mult))
                    k.do("vector", lambda e: e.tensor_tensor(
                        st(ST_MV)[:], st(ST_MV)[:], st(ST_SU)[:], op=OP.add))
                    k.do("vector", lambda e: e.tensor_reduce(
                        st(ST_T4)[:, 0:8],
                        st(ST_MV)[:].rearrange("p (t e) -> p t e", e=8),
                        axis=mybir.AxisListType.X, op=OP.min))
                    k.nop("vector")
                    k.do("vector", lambda e: e.tensor_tensor(
                        st(ST_T4)[:, 0:4], st(ST_T4)[:, 0:4],
                        st(ST_TS)[:, 0:4], op=OP.min))
                    k.nop("vector")
                    k.do("vector", lambda e: e.tensor_copy(
                        st(ST_TS)[:, 0:4], st(ST_T4)[:, 0:4]))
                    k.nop("vector")
                    k.do("vector", lambda e: e.tensor_scalar(
                        st(ST_NTS)[:], st(ST_TS)[:], -1.0, None, op0=OP.mult))
                    vwalk = vfence()
                    vsem = "hra" if Gi % 2 == 0 else "hrb"
                    for j in range(4):
                        h = 4 * hg + j
                        eb = t_eb[j % 2]
                        pb = t_pb[j % 2]
                        k.wait("scalar", "sv", vwalk)
                        k.wait("scalar", "sv", stt_done[j - 2])
                        k.wait("scalar", "sp", tr_pe_par[j % 2])
                        a_x = k.do("scalar", lambda e, j=j, eb=eb, KS=KS, sh=sh:
                                   e.activation(
                            eb[:, 0:KS], sh[j][:, 0:KS], AF.Exp,
                            bias=st(ST_NTS)[:, j:j + 1], scale=1.0), "sa", 1)
                        shat_act[s][j] = a_x
                        k.wait("vector", "sa", a_x)
                        k.wait("vector", "sa", pnorm_done[j - 2])
                        v_p = k.do("vector",
                                   lambda e, j=j, eb=eb, pb=pb, KS=KS, sh=sh:
                                   e.scalar_tensor_tensor(
                            pb[:, 0:KS], sh[j][:, 0:KS],
                            st(ST_TS)[:, j:j + 1], eb[:, 0:KS],
                            op0=OP.is_ge, op1=OP.mult,
                            accum_out=st(ST_SUMP)[:, j:j + 1]), "sv", 1)
                        stt_done[j] = v_p
                        k.nop("vector")
                        vfence()
                        k.do("vector", lambda e, j=j: e.reciprocal(
                            st(ST_RCP)[:, j:j + 1], st(ST_SUMP)[:, j:j + 1]))
                        v_r = vfence()
                        k.wait("scalar", "sv", v_r)
                        a_n = k.do("scalar",
                                   lambda e, j=j, eb=eb, pb=pb, KS=KS:
                                   e.activation(
                            eb[:, 0:KS], pb[:, 0:KS], AF.Identity,
                            bias=0.0, scale=st(ST_RCP)[:, j:j + 1]), "sa", 1)
                        pnorm_done[j] = a_n
                        k.wait("tensor", "sa", a_n)
                        ngrp = (KT4 + 3) // 4
                        for cg in range(ngrp):
                            nc4 = min(4, KT4 - cg * 4)
                            bank = pB[:, (cg % 2) * 512:(cg % 2 + 1) * 512]
                            k.wait("tensor", "sa", pts_bank_evac[cg % 2])
                            for c4 in range(nc4):
                                c = cg * 4 + c4
                                k.do("tensor",
                                     lambda e, bank=bank, c4=c4, c=c, eb=eb:
                                     e.matmul(bank[:, c4 * 128:(c4 + 1) * 128],
                                              eb[:, c * 128:(c + 1) * 128],
                                              g_ident, start=True, stop=True),
                                     "sp", 1 if c4 == nc4 - 1 else None)
                            pe_ptg = k.c["sp"]
                            k.wait("scalar", "sp", pe_ptg)
                            k.wait("scalar", "sp", pts_war_pe)
                            pts_bank_evac[cg % 2] = k.do(
                                "scalar", lambda e, bank=bank, cg=cg, nc4=nc4:
                                e.activation(
                                    t_pts[:, cg * 512:cg * 512 + nc4 * 128],
                                    bank[:, 0:nc4 * 128],
                                    AF.Identity, bias=0.0, scale=1.0), "sa", 1)
                        pts_done = k.c["sa"]
                        k.wait("tensor", "sa", pts_done)
                        k.wait("tensor", "sa", av_evac)
                        k.wait("tensor", vsem, vg_dma[Gi % 2])
                        for c in range(KT4):
                            k.do("tensor", lambda e, c=c, KT4=KT4, sl=Gi % 2:
                                 e.matmul(
                                pB[:, 1536:1664],
                                t_vg[sl][:, c * 128:(c + 1) * 128],
                                t_pts[:, c * 128:(c + 1) * 128],
                                start=(c == 0), stop=(c == KT4 - 1)),
                                "sp", 1 if c == KT4 - 1 else None)
                        av_d = k.c["sp"]
                        pts_war_pe = av_d
                        tr_pe_par[j % 2] = av_d
                        vg_pe[Gi % 2] = av_d
                        k.wait("scalar", "sp", av_d)
                        k.wait("scalar", "otd", ot_dma)
                        av_evac = k.do("scalar", lambda e, h=h: e.activation(
                            t_ot[:, h * 128:(h + 1) * 128], pB[:, 1536:1664],
                            AF.Identity, bias=0.0, scale=1.0), "sa", 1)
                    if hg == 7:
                        k.wait("sync", "sa", k.c["sa"])
                        ot_dma = k.do("sync", lambda e, qt=qt: e.dma_start(
                            out=ot_scr[qt, :, :], in_=t_ot[:]), "otd", 16)
                        if "dbg_shat" in dbg and qt == 7:
                            k.barrier()
                            for j in range(4):
                                k.do("sync", lambda e, j=j, sh=sh: e.dma_start(
                                    out=dbg["dbg_shat"][:, j * 2048:(j + 1) * 2048],
                                    in_=sh[j]), "dbgw", 16)
                            k.do("sync", lambda e: e.dma_start(
                                out=dbg["dbg_state"], in_=g_st[:]), "dbgw", 16)

                qd_ms = 0
                stageA(0)
                for Gi in range(1, len(GL)):
                    stageBC(Gi - 1)
                    stageA(Gi)
                stageBC(len(GL) - 1)
                if "dbg_ot" in dbg:
                    k.barrier()
                    for qt in range(8):
                        k.do("sync", lambda e, qt=qt: e.dma_start(
                            out=dbg["dbg_ot"][qt], in_=ot_scr[qt, :, :]), "dbgw", 16)
            k.barrier()

            if "wo" in PHASES:
                wo_pe = [0, 0]
                ot_pe = [0, 0]
                oev_dma = [0, 0]
                bank_evac = [0, 0]
                for oc in range(8):
                    wb = w_wot[oc % 2]
                    k.wait("gpsimd", "sp", wo_pe[oc % 2])
                    wd = k.do("gpsimd", lambda e, oc=oc, wb=wb: e.dma_start(
                        out=wb.rearrange("p (c o) -> p c o", c=KT),
                        in_=wot[:, oc * 512:(oc + 1) * 512].rearrange(
                            "(c p) o -> p c o", p=128)),
                        "wva" if oc % 2 == 0 else "wvb", 16)
                    for qt in range(8):
                        ob = w_otb[qt % 2]
                        k.wait("sync", "sp", ot_pe[qt % 2])
                        od = k.do("sync", lambda e, qt=qt, ob=ob: e.dma_start(
                            out=ob, in_=ot_scr[qt, :, :]),
                            "otba" if qt % 2 == 0 else "otbb", 16)
                        bank = pA[:, (qt % 2) * 512:(qt % 2 + 1) * 512]
                        k.wait("tensor", "wva" if oc % 2 == 0 else "wvb", wd)
                        k.wait("tensor", "otba" if qt % 2 == 0 else "otbb", od)
                        k.wait("tensor", "sa", bank_evac[qt % 2])
                        for c in range(KT):
                            k.do("tensor", lambda e, bank=bank, c=c, ob=ob, wb=wb:
                                 e.matmul(bank, ob[:, c * 128:(c + 1) * 128],
                                          wb[:, c * 512:(c + 1) * 512],
                                          start=(c == 0), stop=(c == KT - 1)),
                                 "sp", 1 if c == KT - 1 else None)
                        mm_d = k.c["sp"]
                        ot_pe[qt % 2] = mm_d
                        wo_pe[oc % 2] = mm_d
                        ev = w_oev[qt % 2]
                        k.wait("scalar", "sp", mm_d)
                        k.wait("scalar", "oeva" if qt % 2 == 0 else "oevb",
                               oev_dma[qt % 2])
                        bank_evac[qt % 2] = k.do(
                            "scalar", lambda e, bank=bank, ev=ev: e.activation(
                                ev, bank, AF.Identity, bias=0.0, scale=1.0), "sa", 1)
                        k.wait("sync", "sa", bank_evac[qt % 2])
                        oev_dma[qt % 2] = k.do(
                            "sync", lambda e, qt=qt, oc=oc, ev=ev: e.dma_start(
                                out=attn_scr[qt * 128:(qt + 1) * 128,
                                             oc * 512:(oc + 1) * 512], in_=ev),
                            "oeva" if qt % 2 == 0 else "oevb", 16)

            k.barrier()

            if "r1" in PHASES:
                r1_pe = 0
                r1_evac = [0, 0]
                x2dma = 0
                x2tdma = 0
                prev_stats = 0
                buf_war = 0
                for tt in range(8):
                    xa = g_xin[:, 0:D]
                    xb = g_xin[:, D:2 * D]
                    k.wait("sync", "sv", buf_war)
                    da = k.do("sync", lambda e, tt=tt: e.dma_start(
                        out=xa, in_=attn_scr[tt * 128:(tt + 1) * 128, :]), "xa", 16)
                    db = k.do("sync", lambda e, tt=tt: e.dma_start(
                        out=xb, in_=xq[tt * 128:(tt + 1) * 128, :]), "xa", 16)
                    a_r = ln_stats(xa, "xa", db, pe_war=r1_pe)
                    k.wait("vector", "sa", a_r)
                    k.do("vector", lambda e: e.tensor_scalar(
                        g_scr1, xa, g_stats[:, 2:3], g_stats[:, 7:8],
                        op0=OP.subtract, op1=OP.mult))
                    k.wait("vector", "xa", db)
                    k.wait("vector", "sa", prev_stats)
                    k.wait("vector", "x2w", x2dma)
                    v_x2 = k.do("vector", lambda e: e.tensor_tensor(
                        g_scr2, g_scr1, xb, op=OP.add), "sv", 1)
                    buf_war = v_x2
                    k.wait("sync", "sv", v_x2)
                    x2dma = k.do("sync", lambda e, tt=tt: e.dma_start(
                        out=x2_scr[tt * 128:(tt + 1) * 128, :], in_=g_scr2),
                        "x2w", 16)
                    a_r2 = ln_stats(g_scr2, "sv", v_x2, pe_war=r1_pe)
                    prev_stats = a_r2
                    k.wait("vector", "sa", a_r2)
                    k.wait("vector", "sp", r1_pe)
                    v_n2 = k.do("vector", lambda e: e.tensor_scalar(
                        g_lnout[:], g_scr2, g_stats[:, 2:3], g_stats[:, 7:8],
                        op0=OP.subtract, op1=OP.mult), "sv", 1)
                    for e8 in range(8):
                        bank = pB[:, 0:512] if e8 % 2 == 0 else pB[:, 512:1024]
                        k.wait("tensor", "sv", v_n2)
                        k.wait("tensor", "sa", r1_evac[e8 % 2])
                        for c4 in range(4):
                            c = e8 * 4 + c4
                            k.do("tensor", lambda e, bank=bank, c4=c4, c=c:
                                 e.matmul(bank[:, c4 * 128:(c4 + 1) * 128],
                                          g_lnout[:, c * 128:(c + 1) * 128],
                                          g_ident, start=True, stop=True),
                                 "sp", 1 if c4 == 3 else None)
                        r1_pe = k.c["sp"]
                        k.wait("scalar", "sp", r1_pe)
                        k.wait("scalar", "x2tw", x2tdma)
                        a_e = k.do("scalar", lambda e, bank=bank: e.activation(
                            r_x2tb[:], bank, AF.Identity, bias=0.0, scale=1.0),
                            "sa", 1)
                        r1_evac[e8 % 2] = a_e
                        k.wait("sync", "sa", a_e)
                        for c4 in range(4):
                            c = e8 * 4 + c4
                            x2tdma = k.do("sync",
                                          lambda e, c=c, c4=c4, tt=tt: e.dma_start(
                                out=x2t_scr[c, :, tt * 128:(tt + 1) * 128],
                                in_=r_x2tb[:, c4 * 128:(c4 + 1) * 128]), "x2tw", 16)
                if "dbg_x2" in dbg:
                    k.barrier()
                    k.do("sync", lambda e: e.dma_start(
                        out=dbg["dbg_x2"], in_=x2_scr), "dbgw", 16)
                if "dbg_x2t" in dbg:
                    k.barrier()
                    for c in range(KT):
                        k.do("sync", lambda e, c=c: e.dma_start(
                            out=dbg["dbg_x2t"][c], in_=x2t_scr[c, :, :]), "dbgw", 16)

            k.barrier()

            if "b1" in PHASES:
                NFC = FF // 256
                wg_pe = [0, 0]
                g_bank_act = [0, 0]
                u_bank_v = [0, 0]
                gs_v = [0, 0]
                hb_pe = [0, 0]
                htb_dma = [0, 0]
                htr_evac = [0, 0]
                dx = k.do("sync", lambda e: e.dma_start(
                    out=b1_x2t.rearrange("p (c t) -> p c t", c=KT),
                    in_=x2t_scr.rearrange("c p t -> p c t")), "xa", 16)
                k.wait("tensor", "xa", dx)

                def b1_tail(fc, tt):
                    par = tt % 2
                    gbank = pA[:, 0:256] if par == 0 else pA[:, 512:768]
                    ubank = pA[:, 1024:1280] if par == 0 else pA[:, 1536:1792]
                    gs = b1_gs[par]
                    hb = b1_hb[par]
                    k.wait("scalar", "sp", b1_gd[(fc, tt)])
                    k.wait("scalar", "sv", gs_v[par])
                    a_s = k.do("scalar", lambda e, gbank=gbank, gs=gs: e.activation(
                        gs, gbank, AF.Silu, bias=0.0, scale=1.0), "sa", 1)
                    g_bank_act[par] = a_s
                    k.wait("vector", "sa", a_s)
                    k.wait("vector", "sp", b1_ud[(fc, tt)])
                    k.wait("vector", "sp", hb_pe[par])
                    v_h = k.do("vector", lambda e, ubank=ubank, gs=gs, hb=hb:
                               e.tensor_tensor(hb, gs, ubank, op=OP.mult), "sv", 1)
                    gs_v[par] = v_h
                    u_bank_v[par] = v_h
                    b1_vh[(fc, tt)] = v_h

                def b1_tr(fc, tt):
                    par = tt % 2
                    hb = b1_hb[par]
                    htb = b1_htb[par]
                    bank = pB[:, 0:256] if par == 0 else pB[:, 512:768]
                    k.wait("tensor", "sv", b1_vh[(fc, tt)])
                    k.wait("tensor", "sa", htr_evac[par])
                    for c2 in range(2):
                        k.do("tensor", lambda e, c2=c2, hb=hb, bank=bank: e.matmul(
                            bank[:, c2 * 128:(c2 + 1) * 128],
                            hb[:, c2 * 128:(c2 + 1) * 128],
                            g_ident, start=True, stop=True),
                            "sp", 1 if c2 == 1 else None)
                    hb_pe[par] = k.c["sp"]
                    k.wait("scalar", "sp", hb_pe[par])
                    k.wait("scalar", "htw", htb_dma[par])
                    htr_evac[par] = k.do("scalar", lambda e, bank=bank, htb=htb:
                                         e.activation(
                        htb, bank, AF.Identity, bias=0.0, scale=1.0), "sa", 1)
                    k.wait("sync", "sa", htr_evac[par])
                    for c2 in range(2):
                        htb_dma[par] = k.do("sync",
                                            lambda e, fc=fc, c2=c2, tt=tt, htb=htb:
                                            e.dma_start(
                            out=ht_scr[fc * 2 + c2, :, tt * 128:(tt + 1) * 128],
                            in_=htb[:, c2 * 128:(c2 + 1) * 128]), "htw", 16)

                b1_gd = {}
                b1_ud = {}
                b1_vh = {}
                prev = None
                for fc in range(NFC):
                    wgb = b1_wg[fc % 2]
                    wub = b1_wu[fc % 2]
                    k.wait("gpsimd", "sp", wg_pe[fc % 2])
                    dg = k.do("gpsimd", lambda e, fc=fc, wgb=wgb: e.dma_start(
                        out=wgb.rearrange("p (c o) -> p c o", c=KT),
                        in_=wgt[:, fc * 256:(fc + 1) * 256].rearrange(
                            "(c p) o -> p c o", p=128)),
                        "wva" if fc % 2 == 0 else "wvb", 16)
                    du = k.do("gpsimd", lambda e, fc=fc, wub=wub: e.dma_start(
                        out=wub.rearrange("p (c o) -> p c o", c=KT),
                        in_=wut[:, fc * 256:(fc + 1) * 256].rearrange(
                            "(c p) o -> p c o", p=128)),
                        "wua" if fc % 2 == 0 else "wub", 16)
                    k.wait("tensor", "wva" if fc % 2 == 0 else "wvb", dg)
                    k.wait("tensor", "wua" if fc % 2 == 0 else "wub", du)
                    for tt in range(8):
                        par = tt % 2
                        gbank = pA[:, 0:256] if par == 0 else pA[:, 512:768]
                        ubank = pA[:, 1024:1280] if par == 0 else pA[:, 1536:1792]
                        k.wait("tensor", "sa", g_bank_act[par])
                        k.wait("tensor", "sv", u_bank_v[par])
                        for c in range(KT):
                            k.do("tensor", lambda e, gbank=gbank, c=c, tt=tt, wgb=wgb:
                                 e.matmul(gbank,
                                          b1_x2t[:, c * NQ + tt * 128:
                                                 c * NQ + (tt + 1) * 128],
                                          wgb[:, c * 256:(c + 1) * 256],
                                          start=(c == 0), stop=(c == KT - 1)),
                                 "sp", 1 if c == KT - 1 else None)
                        b1_gd[(fc, tt)] = k.c["sp"]
                        for c in range(KT):
                            k.do("tensor", lambda e, ubank=ubank, c=c, tt=tt, wub=wub:
                                 e.matmul(ubank,
                                          b1_x2t[:, c * NQ + tt * 128:
                                                 c * NQ + (tt + 1) * 128],
                                          wub[:, c * 256:(c + 1) * 256],
                                          start=(c == 0), stop=(c == KT - 1)),
                                 "sp", 1 if c == KT - 1 else None)
                        b1_ud[(fc, tt)] = k.c["sp"]
                        if tt == 7:
                            wg_pe[fc % 2] = k.c["sp"]
                        b1_tail(fc, tt)
                        if prev is not None:
                            b1_tr(*prev)
                        prev = (fc, tt)
                b1_tr(*prev)
                if "dbg_ht" in dbg:
                    k.barrier()
                    for c in range(FF // 128):
                        k.do("sync", lambda e, c=c: e.dma_start(
                            out=dbg["dbg_ht"][c], in_=ht_scr[c, :, :]), "dbgw", 16)

            k.barrier()

            if "b2" in PHASES:
                NF128 = FF // 128
                FCG = [(i * 8, min(8, NF128 - i * 8)) for i in range((NF128 + 7) // 8)]
                wd_pe = [0, 0]
                ht_pe = [0, 0]
                ev_dma = 0
                bank_act = 0
                for oc in range(8):
                    for gi, (f0, nf) in enumerate(FCG):
                        wdb = b2_wd[gi % 2]
                        htb = b2_ht[gi % 2]
                        k.wait("gpsimd", "sp", wd_pe[gi % 2])
                        dwd = k.do("gpsimd",
                                   lambda e, f0=f0, nf=nf, oc=oc, wdb=wdb:
                                   e.dma_start(
                            out=wdb.rearrange("p (c o) -> p c o", c=8)[:, 0:nf, :],
                            in_=wdt[f0 * 128:(f0 + nf) * 128,
                                    oc * 512:(oc + 1) * 512].rearrange(
                                "(c p) o -> p c o", p=128)),
                            "wva" if gi % 2 == 0 else "wvb", 16)
                        k.wait("sync", "sp", ht_pe[gi % 2])
                        dht = k.do("sync", lambda e, f0=f0, nf=nf, htb=htb:
                                   e.dma_start(
                            out=htb.rearrange("p (c t) -> p c t", c=8)[:, 0:nf, :],
                            in_=ht_scr[f0:f0 + nf, :, :].rearrange(
                                "c p t -> p c t")),
                            "hra" if gi % 2 == 0 else "hrb", 16)
                        k.wait("tensor", "wva" if gi % 2 == 0 else "wvb", dwd)
                        k.wait("tensor", "hra" if gi % 2 == 0 else "hrb", dht)
                        if gi == 0:
                            k.wait("tensor", "sa", bank_act)
                        for ci in range(nf):
                            for tt in range(8):
                                bank = (pA if tt < 4 else pB)[
                                    :, (tt % 4) * 512:(tt % 4 + 1) * 512]
                                last = (gi == len(FCG) - 1 and ci == nf - 1)
                                k.do("tensor",
                                     lambda e, bank=bank, ci=ci, tt=tt, htb=htb,
                                     wdb=wdb, gi=gi, last=last:
                                     e.matmul(
                                         bank,
                                         htb[:, ci * NQ + tt * 128:
                                             ci * NQ + (tt + 1) * 128],
                                         wdb[:, ci * 512:(ci + 1) * 512],
                                         start=(gi == 0 and ci == 0), stop=last),
                                     "sp", 1 if (tt == 7 and ci == nf - 1) else None)
                        mm_d = k.c["sp"]
                        wd_pe[gi % 2] = mm_d
                        ht_pe[gi % 2] = mm_d
                    all_d = k.c["sp"]
                    for tt in range(8):
                        bank = (pA if tt < 4 else pB)[
                            :, (tt % 4) * 512:(tt % 4 + 1) * 512]
                        k.wait("scalar", "sp", all_d)
                        k.wait("scalar", "yw", ev_dma)
                        a_e = k.do("scalar", lambda e, bank=bank: e.activation(
                            b2_yev[:, 0:512], bank, AF.Identity, bias=0.0,
                            scale=1.0), "sa", 1)
                        k.wait("sync", "sa", a_e)
                        ev_dma = k.do("sync", lambda e, tt=tt, oc=oc: e.dma_start(
                            out=y_scr[tt * 128:(tt + 1) * 128,
                                      oc * 512:(oc + 1) * 512],
                            in_=b2_yev[:, 0:512]), "yw", 16)
                    bank_act = k.c["sa"]
                if "dbg_y" in dbg:
                    k.barrier()
                    k.do("sync", lambda e: e.dma_start(
                        out=dbg["dbg_y"], in_=y_scr), "dbgw", 16)

            k.barrier()

            if "r2" in PHASES:
                out_dma = 0
                buf_war = 0
                prev_stats = 0
                for tt in range(8):
                    xa = g_xin[:, 0:D]
                    xb = g_xin[:, D:2 * D]
                    k.wait("sync", "sv", buf_war)
                    da = k.do("sync", lambda e, tt=tt: e.dma_start(
                        out=xa, in_=y_scr[tt * 128:(tt + 1) * 128, :]), "xa", 16)
                    db = k.do("sync", lambda e, tt=tt: e.dma_start(
                        out=xb, in_=x2_scr[tt * 128:(tt + 1) * 128, :]), "xa", 16)
                    a_r = ln_stats(xa, "xa", db)
                    k.wait("vector", "sa", a_r)
                    k.wait("vector", "outw", out_dma)
                    k.do("vector", lambda e: e.tensor_scalar(
                        g_scr1, xa, g_stats[:, 2:3], g_stats[:, 7:8],
                        op0=OP.subtract, op1=OP.mult))
                    k.wait("vector", "xa", db)
                    buf_war = k.do("vector", lambda e: e.tensor_tensor(
                        g_scr2, g_scr1, xb, op=OP.add), "sv", 1)
                    k.wait("sync", "sv", buf_war)
                    out_dma = k.do("sync", lambda e, tt=tt: e.dma_start(
                        out=out_d[tt * 128:(tt + 1) * 128, :], in_=g_scr2),
                        "outw", 16)
                k.wait("sync", "outw", k.c["outw"])

        kb = KB(blk, sems)
        kb.run(script)
    return nc


def _host_tables():
    t = np.arange(S, dtype=np.float32)
    inv = (1.0 / (10000.0 ** (np.arange(0, HD, 2).astype(np.float32) / HD))).astype(np.float32)
    fr = np.outer(t, inv).astype(np.float32)
    emb = np.concatenate([fr, fr], axis=1)
    cos = np.cos(emb).astype(np.float32)
    sin = np.sin(emb).astype(np.float32)
    sinsig = np.concatenate([-sin[:, :64], sin[:, 64:]], axis=1).astype(np.float32)
    return cos, sinsig


def _core_meta(core):
    b = core // 2
    blocks = (0, 3) if core % 2 == 0 else (1, 2)
    qpos = np.concatenate([np.arange(blocks[0] * 512, blocks[0] * 512 + 512),
                           np.arange(blocks[1] * 512, blocks[1] * 512 + 512)])
    return b, blocks, qpos


def _in_maps(hs, wqt, wkt, wvt, wot, wgt, wut, wdt):
    cos, sinsig = _host_tables()
    cosk = np.ascontiguousarray(cos.reshape(16, 128, HD).transpose(1, 0, 2))
    sink = np.ascontiguousarray(sinsig.reshape(16, 128, HD).transpose(1, 0, 2))
    iota = np.broadcast_to(np.arange(S, dtype=np.uint16), (128, S)).copy()
    ident = np.eye(128, dtype=np.float32).astype(ml_dtypes.bfloat16)
    maps = []
    for core in range(8):
        b, blocks, qpos = _core_meta(core)
        xq = np.ascontiguousarray(hs[b][qpos])
        xkv = np.ascontiguousarray(hs[b])
        cq = np.ascontiguousarray(cos[qpos].reshape(8, 128, HD).transpose(1, 0, 2))
        sq = np.ascontiguousarray(sinsig[qpos].reshape(8, 128, HD).transpose(1, 0, 2))
        me = (qpos + 1).astype(np.float32)
        targ = np.minimum(MASK_K, qpos + 1).astype(np.float32)
        ks_row = np.repeat(np.array(KS_OF_QT, np.float32), 128)
        t2 = 2.0 * targ - ks_row
        cst = np.zeros((128, 104), np.float32)
        cst[:, 0:8] = me.reshape(8, 128).T
        cst[:, 8:16] = targ.reshape(8, 128).T
        cst[:, 16:24] = t2.reshape(8, 128).T
        cst[:, 24:88] = np.broadcast_to(
            np.tile(np.arange(1, 9, dtype=np.float32), 8), (128, 64))
        cst[:, 88:96] = (targ / 4.0).reshape(8, 128).T
        cst[:, 96:104] = (t2 / 4.0).reshape(8, 128).T
        maps.append(dict(
            xq=xq, xkv=xkv, wqt=wqt, wkt=wkt, wvt=wvt, wot=wot,
            wgt=wgt, wut=wut, wdt=wdt,
            cosq=cq, sinq=sq, cosk=cosk, sink=sink,
            consts=cst, iota_in=iota, ident_in=ident))
    return maps


def kernel(L0_attn_hidden_states, L0_attn_Wq, L0_attn_Wk, L0_attn_Wv, L0_attn_Wo,
           L0_ffn_W_gate, L0_ffn_W_up, L0_ffn_W_down):
    hs = np.ascontiguousarray(L0_attn_hidden_states, dtype=np.float32)
    maps = _in_maps(
        hs,
        np.ascontiguousarray(L0_attn_Wq.T, dtype=np.float32),
        np.ascontiguousarray(L0_attn_Wk.T, dtype=np.float32),
        np.ascontiguousarray(L0_attn_Wv.T, dtype=np.float32),
        np.ascontiguousarray(L0_attn_Wo.T, dtype=np.float32),
        np.ascontiguousarray(L0_ffn_W_gate.T, dtype=np.float32),
        np.ascontiguousarray(L0_ffn_W_up.T, dtype=np.float32),
        np.ascontiguousarray(L0_ffn_W_down.T, dtype=np.float32))
    nc = bass.Bass("TRN2", target_bir_lowering=False)
    build(nc)
    res = run_bass_kernel_spmd(nc, maps, core_ids=list(range(8)))
    out = np.empty((NB, S, D), np.float32)
    for core in range(8):
        b, blocks, _ = _core_meta(core)
        o = res.results[core]["out"]
        out[b, blocks[0] * 512:blocks[0] * 512 + 512] = o[0:512]
        out[b, blocks[1] * 512:blocks[1] * 512 + 512] = o[512:1024]
    return out


# version 10 (final)
# speedup vs baseline: 1.0323x; 1.0323x over previous
import numpy as np
import ml_dtypes
import concourse.bass as bass
import concourse.mybir as mybir
from concourse.bass_utils import run_bass_kernel_spmd

F32 = mybir.dt.float32
BF16 = mybir.dt.bfloat16
U16 = mybir.dt.uint16
AF = mybir.ActivationFunctionType
OP = mybir.AluOpType

D = 4096
S = 2048
NB = 4
H = 32
HKV = 8
HD = 128
FF = 11008
NQ = 1024
KT = 32
T0 = 57.5
W0 = 12.5
NSUB = 5
NFULL = 10
WFULL0 = 3.2
W_SUB = [W0 * (0.5 ** (i + 1)) for i in range(NSUB)]
W_FULL = [WFULL0 * (0.5 ** (i + 1)) for i in range(NFULL)]
KS_OF_QT = [640, 768, 896, 1024, 1664, 1792, 1920, 2048]
SHIFT = 60.0
BIG = 1.0e38
INV_SQRT_HD = float(np.float32(1.0) / np.float32(np.sqrt(np.float32(HD))))
EPS = 1e-5
E_OF_QT = [2, 2, 2, 2, 4, 4, 4, 4]
MASK_K = 205

ENGINES = ["sync", "gpsimd", "scalar", "vector", "tensor"]

DEBUG_OUTS = []
PHASES = ["p0", "attn", "wo", "r1", "b1", "b2", "r2"]


class KB:

    def __init__(self, blk, sems):
        self.blk = blk
        self.sems = sems
        self.cur = None
        self.E = None
        self.c = {}

    def run(self, script):
        for name in ENGINES:
            self.cur = name
            for kk in self.sems:
                self.c[kk] = 0

            def body(e):
                self.E = e
                script(self)

            getattr(self.blk, name)(body)

    def do(self, eng, fn, inc=None, n=1):
        if n is None:
            inc = None
        if self.cur == eng:
            inst = fn(self.E)
            if inc is not None:
                inst.then_inc(self.sems[inc], n)
        if inc is not None:
            self.c[inc] += n
            return self.c[inc]
        return None

    def wait(self, eng, ctr, val):
        if val is None or val <= 0:
            return
        if self.cur == eng:
            self.E.wait_ge(self.sems[ctr], val)

    def nop(self, eng):
        if self.cur == eng:
            self.E.engine_nop()

    def barrier(self):
        snap = dict(self.c)
        for eng in ENGINES:
            for ctr, val in snap.items():
                self.wait(eng, ctr, val)


class Arena:
    def __init__(self, ap, words):
        self.ap = ap
        self.words = words
        self.off = 0
        self.marks = []

    def f32(self, w):
        assert self.off + w <= self.words, f"arena overflow {self.off}+{w}"
        r = self.ap[:, self.off:self.off + w]
        self.off += w
        return r

    def bf16(self, n):
        assert n % 2 == 0
        return self.f32(n // 2).bitcast(BF16)

    def u16(self, n):
        assert n % 2 == 0
        return self.f32(n // 2).bitcast(U16)

    def push(self):
        self.marks.append(self.off)

    def pop(self):
        self.off = self.marks.pop()


def build(nc):
    dt = nc.dram_tensor
    xq = dt("xq", [NQ, D], F32, kind="ExternalInput").ap()
    xkv = dt("xkv", [S, D], F32, kind="ExternalInput").ap()
    wqt = dt("wqt", [D, D], F32, kind="ExternalInput").ap()
    wkt = dt("wkt", [D, HKV * HD], F32, kind="ExternalInput").ap()
    wvt = dt("wvt", [D, HKV * HD], F32, kind="ExternalInput").ap()
    wot = dt("wot", [D, D], F32, kind="ExternalInput").ap()
    wgt = dt("wgt", [D, FF], F32, kind="ExternalInput").ap()
    wut = dt("wut", [D, FF], F32, kind="ExternalInput").ap()
    wdt = dt("wdt", [FF, D], F32, kind="ExternalInput").ap()
    cosq = dt("cosq", [128, 8, HD], F32, kind="ExternalInput").ap()
    sinq = dt("sinq", [128, 8, HD], F32, kind="ExternalInput").ap()
    cosk = dt("cosk", [128, 16, HD], F32, kind="ExternalInput").ap()
    sink = dt("sink", [128, 16, HD], F32, kind="ExternalInput").ap()
    consts = dt("consts", [128, 104], F32, kind="ExternalInput").ap()
    iota_in = dt("iota_in", [128, S], U16, kind="ExternalInput").ap()
    ident_in = dt("ident_in", [128, 128], BF16, kind="ExternalInput").ap()

    qt_scr = dt("qt_scr", [H, 128, NQ], BF16, kind="Internal").ap()
    v_scr = dt("v_scr", [16, 128, HKV * HD], BF16, kind="Internal").ap()
    wot_bf = dt("wot_bf", [D, D], BF16, kind="Internal").ap()
    ot_scr = dt("ot_scr", [8, 128, D], BF16, kind="Internal").ap()
    attn_scr = dt("attn_scr", [NQ, D], F32, kind="Internal").ap()
    x2_scr = dt("x2_scr", [NQ, D], F32, kind="Internal").ap()
    x2t_scr = dt("x2t_scr", [KT, 128, NQ], BF16, kind="Internal").ap()
    ht_scr = dt("ht_scr", [FF // 128, 128, NQ], BF16, kind="Internal").ap()
    y_scr = dt("y_scr", [NQ, D], F32, kind="Internal").ap()
    out_d = dt("out", [NQ, D], F32, kind="ExternalOutput").ap()

    dbg = {}
    for nm, shape, dty in [
        ("dbg_ktr", [128, HKV * S], BF16), ("dbg_vtr", [128, 16 * HKV * HD], BF16),
        ("dbg_qt", [H, 128, NQ], BF16), ("dbg_shat", [128, 4 * 2048], F32),
        ("dbg_state", [128, 64 * 16], F32), ("dbg_ot", [8, 128, D], BF16),
        ("dbg_attn", [NQ, D], F32), ("dbg_x2", [NQ, D], F32),
        ("dbg_ht", [FF // 128, 128, NQ], BF16), ("dbg_y", [NQ, D], F32),
        ("dbg_x2t", [KT, 128, NQ], BF16),
    ]:
        if nm in DEBUG_OUTS:
            dbg[nm] = dt(nm, shape, dty, kind="ExternalOutput").ap()

    AW = 51500
    SEM_NAMES = ["sp", "sa", "sv", "init", "cs", "xa", "xb", "qtb",
                 "wva", "wvb", "wua", "wub", "qev", "otd", "otba", "otbb",
                 "oeva", "oevb", "x2w", "x2tw", "htw", "hra", "hrb", "yw",
                 "outw", "dbgw"]
    from contextlib import ExitStack as _ES
    _es_sem = _ES()
    with (
        nc.sbuf_tensor("arena", [128, AW], F32) as arena_t,
        nc.psum_tensor("pA", [128, 2048], F32) as pA,
        nc.psum_tensor("pB", [128, 2048], F32) as pB,
        _es_sem,
    ):
        sems = {nm: _es_sem.enter_context(nc.semaphore(nm)) for nm in SEM_NAMES}
        blk = _es_sem.enter_context(nc.Block())
        ar = Arena(arena_t[:], AW)
        g_ident = ar.bf16(128)
        g_iota = ar.u16(2048)
        g_cst = ar.f32(104)
        g_stats = ar.f32(64)
        g_st = ar.f32(17 * 64)
        g_xin = ar.f32(2 * D)
        g_junkb = ar.bf16(2048)
        g_junkc = g_junkb
        g_lnout = ar.bf16(4096)

        def st(i):
            return g_st[:, i * 64:(i + 1) * 64]
        (ST_TC, ST_NG, ST_CNT, ST_P, ST_SU, ST_THR, ST_D, ST_TS, ST_NTS,
         ST_CF, ST_HALF, ST_DREP, ST_MV, ST_T4, ST_SUMP, ST_RCP,
         ST_THRQ) = range(17)

        ar.push()
        a_ktr = ar.bf16(HKV * S)
        ar.push()
        p_ln1t = ar.bf16(KT * 512)
        p_wbuf = [ar.bf16(16 * 512), ar.bf16(16 * 512)]
        p_cost = ar.f32(4 * HD)
        p_sint = ar.f32(4 * HD)
        p_ktmp = [ar.f32(512), ar.f32(512)]
        p_rt1 = ar.f32(512)
        p_rt2 = ar.f32(512)
        p_krot = [ar.bf16(512), ar.bf16(512)]
        p_qev = ar.bf16(512)
        ar.pop()
        ar.push()
        t_shat = [[ar.f32(2048) for _ in range(4)] for _ in range(2)]
        t_qtb = ar.bf16(H * 128)
        t_j60 = ar.bf16(2048)
        t_pb = [ar.bf16(2048), ar.bf16(2048)]
        t_eb = [ar.bf16(2048), ar.bf16(2048)]
        t_pts = ar.bf16(16 * 128)
        t_ot = ar.bf16(H * 128)
        t_m = g_xin[:, 0:2048]
        t_vg = [ar.bf16(2048), ar.bf16(2048)]
        ar.pop()
        ar.pop()

        ar.push()
        w_wot = [ar.bf16(KT * 512), ar.bf16(KT * 512)]
        w_otb = ar.bf16(D)
        w_attn = ar.f32(D)
        g_scr1 = ar.f32(D)
        g_scr2 = ar.f32(D)
        r_x2tb = ar.bf16(4 * 128)
        ar.pop()
        ar.push()
        b1_wg = [ar.bf16(KT * 256), ar.bf16(KT * 256)]
        b1_wu = [ar.bf16(KT * 256), ar.bf16(KT * 256)]
        b1_x2t = ar.bf16(KT * NQ)
        b1_gs = [ar.bf16(256), ar.bf16(256)]
        b1_hb = [ar.bf16(256), ar.bf16(256)]
        b1_htb = [ar.bf16(256), ar.bf16(256)]
        ar.pop()
        ar.push()
        b2_wd = [ar.bf16(8 * 512), ar.bf16(8 * 512)]
        b2_ht = [ar.bf16(8 * NQ), ar.bf16(8 * NQ)]
        b2_yev = ar.f32(D)
        ar.pop()

        def script(k):
            k.do("sync", lambda e: e.dma_start(out=g_ident, in_=ident_in), "init", 16)
            k.do("sync", lambda e: e.dma_start(out=g_iota, in_=iota_in), "init", 16)
            init_d = k.do("sync", lambda e: e.dma_start(out=g_cst, in_=consts), "init", 16)
            HB = ST_HALF * 64
            k.do("vector", lambda e: e.memset(g_st[:, 0:HB], 0.0))
            k.do("vector", lambda e: e.memset(st(ST_HALF)[:, 0:2], 0.5))
            k.do("vector", lambda e: e.memset(st(ST_HALF)[:, 2:4], 1.0))
            k.do("vector", lambda e: e.memset(st(ST_HALF)[:, 4:64], 0.0))
            k.do("vector", lambda e: e.memset(g_st[:, HB + 64:], 0.0), "sv", 1)
            for eng in ("scalar", "vector", "tensor", "gpsimd"):
                k.wait(eng, "init", init_d)

            def afence():
                return k.do("scalar", lambda e: e.activation(
                    g_junkb[:, 0:64], g_junkb[:, 0:64], AF.Identity,
                    bias=0.0, scale=1.0), "sa", 1)

            def vfence():
                return k.do("vector", lambda e: e.tensor_copy(
                    g_junkc[:, 64:128], g_junkc[:, 0:64]), "sv", 1)

            def ln_stats(xv, wait_ctr=None, wait_val=None, pe_war=0):
                k.wait("scalar", "sp", pe_war)
                if wait_ctr is not None:
                    k.wait("scalar", wait_ctr, wait_val)
                k.do("scalar", lambda e: e.activation(
                    g_lnout[:], xv, AF.Identity, bias=0.0, scale=1.0,
                    accum_out=g_stats[:, 0:1]))
                k.do("scalar", lambda e: e.activation(
                    g_lnout[:], xv, AF.Square, bias=0.0, scale=1.0,
                    accum_out=g_stats[:, 1:2]))
                a_s = afence()
                k.wait("vector", "sa", a_s)
                k.do("vector", lambda e: e.tensor_scalar(
                    g_stats[:, 2:3], g_stats[:, 0:1], 1.0 / D, None, op0=OP.mult))
                k.do("vector", lambda e: e.tensor_scalar(
                    g_stats[:, 3:4], g_stats[:, 1:2], 1.0 / D, EPS,
                    op0=OP.mult, op1=OP.add))
                k.nop("vector")
                k.do("vector", lambda e: e.tensor_tensor(
                    g_stats[:, 4:5], g_stats[:, 2:3], g_stats[:, 2:3], op=OP.mult))
                k.nop("vector")
                v_v = k.do("vector", lambda e: e.tensor_tensor(
                    g_stats[:, 5:6], g_stats[:, 3:4], g_stats[:, 4:5],
                    op=OP.subtract), "sv", 1)
                k.wait("scalar", "sv", v_v)
                k.do("scalar", lambda e: e.activation(
                    g_stats[:, 6:7], g_stats[:, 5:6], AF.Ln, bias=0.0, scale=1.0))
                k.do("scalar", lambda e: e.activation(
                    g_stats[:, 7:8], g_stats[:, 6:7], AF.Exp, bias=0.0, scale=-0.5))
                return afence()

            ktr_done = 0
            vtr_done = 0
            qscr_done = 0
            if "p0" in PHASES:
                tr_evac = [0, 0]
                ln1t_pe = 0
                wave_pe = [0, 0]
                proj_evac = [0, 0, 0, 0]
                ktmp_v = 0
                krot_pe = 0
                rope_par = 0
                ktmp_free = [0, 0]
                krot_free = [0, 0]
                kq_evac = 0
                qev_dma = 0
                groups = [("kv", g) for g in range(4)] + [("q", g) for g in range(2)]
                for (kind, g) in groups:
                    src = xkv if kind == "kv" else xq
                    cosg = cosk if kind == "kv" else cosq
                    sing = sink if kind == "kv" else sinq
                    gt0 = g * 4
                    k.wait("sync", "sv", ktmp_v)
                    k.do("sync", lambda e, gt0=gt0, cosg=cosg: e.dma_start(
                        out=p_cost.rearrange("p (t d) -> p t d", t=4),
                        in_=cosg[:, gt0:gt0 + 4, :]), "cs", 16)
                    cs_d = k.do("sync", lambda e, gt0=gt0, sing=sing: e.dma_start(
                        out=p_sint.rearrange("p (t d) -> p t d", t=4),
                        in_=sing[:, gt0:gt0 + 4, :]), "cs", 16)
                    ln_pe_hist = {}
                    for tt in range(4):
                        row0 = (gt0 + tt) * 128
                        xv = g_xin[:, (tt % 2) * D:(tt % 2 + 1) * D]
                        xsem = "xa" if tt % 2 == 0 else "xb"
                        if tt >= 2:
                            k.wait("sync", "sv", ln_v_hist[tt - 2])
                        dma_d = k.do("sync", lambda e, xv=xv, row0=row0, src=src:
                                     e.dma_start(out=xv, in_=src[row0:row0 + 128, :]),
                                     xsem, 16)
                        a_r = ln_stats(xv, xsem, dma_d, pe_war=ln1t_pe)
                        k.wait("vector", "sa", a_r)
                        k.wait("vector", "sp", ln1t_pe)
                        v_n = k.do("vector", lambda e, xv=xv: e.tensor_scalar(
                            g_lnout[:], xv, g_stats[:, 2:3], g_stats[:, 7:8],
                            op0=OP.subtract, op1=OP.mult), "sv", 1)
                        if tt == 0:
                            ln_v_hist = {}
                        ln_v_hist[tt] = v_n
                        for e8 in range(8):
                            bank = pB[:, 0:512] if e8 % 2 == 0 else pB[:, 512:1024]
                            k.wait("tensor", "sv", v_n)
                            k.wait("tensor", "sa", tr_evac[e8 % 2])
                            for c4 in range(4):
                                c = e8 * 4 + c4
                                k.do("tensor", lambda e, bank=bank, c4=c4, c=c:
                                     e.matmul(bank[:, c4 * 128:(c4 + 1) * 128],
                                              g_lnout[:, c * 128:(c + 1) * 128],
                                              g_ident, start=True, stop=True),
                                     "sp", 1 if c4 == 3 else None)
                            ln1t_pe = k.c["sp"]
                            k.wait("scalar", "sp", ln1t_pe)
                            dst = p_ln1t.rearrange("p (c t) -> p c t", c=KT)[
                                :, e8 * 4:e8 * 4 + 4, tt * 128:(tt + 1) * 128]
                            tr_evac[e8 % 2] = k.do(
                                "scalar", lambda e, bank=bank, dst=dst: e.activation(
                                    dst, bank.rearrange("p (c t) -> p c t", c=4),
                                    AF.Identity, bias=0.0, scale=1.0), "sa", 1)
                    ln_done_a = k.c["sa"]
                    projs = ([("k", wkt, 2), ("v", wvt, 2)] if kind == "kv"
                             else [("q", wqt, 8)])
                    for (pname, wsrc, nocs) in projs:
                        for oc in range(nocs):
                            def wave_dma(w):
                                wb = p_wbuf[w % 2]
                                k.wait("gpsimd", "sp", wave_pe[w % 2])
                                return k.do(
                                    "gpsimd", lambda e, w=w, wb=wb, wsrc=wsrc, oc=oc:
                                    e.dma_start(
                                        out=wb.rearrange("p (c o) -> p c o", c=16),
                                        in_=wsrc[w * 2048:(w + 1) * 2048,
                                                 oc * 512:(oc + 1) * 512].rearrange(
                                            "(c p) o -> p c o", p=128)),
                                    "wva" if w % 2 == 0 else "wvb", 16)

                            def wave_mm(w, wd):
                                wb = p_wbuf[w % 2]
                                k.wait("tensor", "wva" if w % 2 == 0 else "wvb", wd)
                                for tt in range(4):
                                    for c8 in range(16):
                                        c = w * 16 + c8
                                        k.do("tensor",
                                             lambda e, tt=tt, c=c, c8=c8, wb=wb:
                                             e.matmul(
                                                 pA[:, tt * 512:(tt + 1) * 512],
                                                 p_ln1t[:, c * 512 + tt * 128:
                                                        c * 512 + (tt + 1) * 128],
                                                 wb[:, c8 * 512:(c8 + 1) * 512],
                                                 start=(c == 0), stop=(c == 31)),
                                             "sp", 1 if (tt == 3 and c8 == 15) else None)
                                wave_pe[w % 2] = k.c["sp"]

                            wd0 = wave_dma(0)
                            wd1 = wave_dma(1)
                            k.wait("tensor", "sa", ln_done_a)
                            k.wait("tensor", "sa", max(proj_evac))
                            wave_mm(0, wd0)
                            wave_mm(1, wd1)
                            mm_d = k.c["sp"]
                            for tt in range(4):
                                bank = pA[:, tt * 512:(tt + 1) * 512]
                                kt_i = gt0 + tt
                                if pname == "v":
                                    k.wait("scalar", "sp", mm_d)
                                    k.wait("scalar", "qev", qev_dma)
                                    a_v = k.do("scalar",
                                               lambda e, bank=bank: e.activation(
                                        p_qev, bank, AF.Identity, bias=0.0,
                                        scale=1.0), "sa", 1)
                                    proj_evac[tt] = a_v
                                    k.wait("sync", "sa", a_v)
                                    qev_dma = k.do("sync",
                                                   lambda e, kt_i=kt_i, oc=oc:
                                                   e.dma_start(
                                        out=v_scr[kt_i, :, oc * 512:(oc + 1) * 512],
                                        in_=p_qev), "qev", 16)
                                    vtr_done = a_v
                                    continue
                                par = rope_par
                                ktmp = p_ktmp[par]
                                krot = p_krot[par]
                                k.wait("scalar", "sp", mm_d)
                                k.wait("scalar", "sv", ktmp_free[par])
                                a_e = k.do("scalar", lambda e, bank=bank, ktmp=ktmp:
                                           e.activation(
                                    ktmp, bank, AF.Identity, bias=0.0, scale=1.0),
                                    "sa", 1)
                                proj_evac[tt] = a_e
                                k.wait("vector", "sa", a_e)
                                k.wait("vector", "cs", cs_d)
                                k.wait("vector", "sp", krot_free[par])
                                c4v = p_cost.rearrange("p (t d) -> p t d", t=4)[
                                    :, tt:tt + 1, :].to_broadcast([128, 4, HD])
                                k.do("vector", lambda e, c4v=c4v, ktmp=ktmp:
                                     e.tensor_tensor(
                                    p_rt1.rearrange("p (h d) -> p h d", h=4),
                                    ktmp.rearrange("p (h d) -> p h d", h=4),
                                    c4v, op=OP.mult))
                                s8 = p_sint.rearrange("p (t two d) -> p t two d",
                                                      t=4, two=2)
                                slo = s8[:, tt:tt + 1, 0:1, :].rearrange(
                                    "p a b d -> p (a b) d").to_broadcast([128, 4, 64])
                                shi = s8[:, tt:tt + 1, 1:2, :].rearrange(
                                    "p a b d -> p (a b) d").to_broadcast([128, 4, 64])
                                k8 = ktmp.rearrange("p (h two d) -> p h two d",
                                                    h=4, two=2)
                                t28 = p_rt2.rearrange("p (h two d) -> p h two d",
                                                      h=4, two=2)
                                k.do("vector", lambda e, k8=k8, t28=t28, slo=slo:
                                     e.tensor_tensor(t28[:, :, 0, :], k8[:, :, 1, :],
                                                     slo, op=OP.mult))
                                k.do("vector", lambda e, k8=k8, t28=t28, shi=shi:
                                     e.tensor_tensor(t28[:, :, 1, :], k8[:, :, 0, :],
                                                     shi, op=OP.mult))
                                ktmp_v = k.do("vector", lambda e, krot=krot:
                                              e.tensor_tensor(
                                    krot, p_rt1, p_rt2, op=OP.add), "sv", 1)
                                ktmp_free[par] = ktmp_v
                                k.wait("tensor", "sv", ktmp_v)
                                k.wait("tensor", "sa", kq_evac)
                                for hh in range(4):
                                    k.do("tensor", lambda e, hh=hh, krot=krot:
                                         e.matmul(
                                        pB[:, 1024 + hh * 128:1024 + (hh + 1) * 128],
                                        krot[:, hh * 128:(hh + 1) * 128],
                                        g_ident, start=True, stop=True),
                                        "sp", 1 if hh == 3 else None)
                                krot_free[par] = k.c["sp"]
                                krot_pe = k.c["sp"]
                                rope_par = 1 - rope_par
                                k.wait("scalar", "sp", krot_pe)
                                if pname == "k":
                                    dstk = a_ktr.rearrange("p (h t) -> p h t", h=HKV)[
                                        :, 4 * oc:4 * oc + 4,
                                        kt_i * 128:(kt_i + 1) * 128]
                                    kq_evac = k.do("scalar",
                                                   lambda e, dstk=dstk: e.activation(
                                        dstk,
                                        pB[:, 1024:1536].rearrange("p (h t) -> p h t", h=4),
                                        AF.Identity,
                                        bias=0.0, scale=1.0), "sa", 1)
                                    ktr_done = kq_evac
                                else:
                                    k.wait("scalar", "qev", qev_dma)
                                    kq_evac = k.do("scalar", lambda e: e.activation(
                                        p_qev, pB[:, 1024:1536], AF.Identity,
                                        bias=0.0, scale=1.0), "sa", 1)
                                    k.wait("sync", "sa", kq_evac)
                                    ttq = gt0 + tt
                                    for hh in range(4):
                                        qev_dma = k.do(
                                            "sync", lambda e, hh=hh, oc=oc, ttq=ttq:
                                            e.dma_start(
                                                out=qt_scr[4 * oc + hh, :,
                                                           ttq * 128:(ttq + 1) * 128],
                                                in_=p_qev[:, hh * 128:(hh + 1) * 128]),
                                            "qev", 16)
                                    qscr_done = qev_dma
                if "dbg_ktr" in dbg:
                    k.barrier()
                    k.do("sync", lambda e: e.dma_start(
                        out=dbg["dbg_ktr"], in_=a_ktr), "dbgw", 16)

                if "dbg_qt" in dbg:
                    k.barrier()
                    for hh in range(H):
                        k.do("sync", lambda e, hh=hh: e.dma_start(
                            out=dbg["dbg_qt"][hh], in_=qt_scr[hh, :, :]), "dbgw", 16)

            k.barrier()

            if "attn" in PHASES:
                pa_free = [0, 0]
                shat_act = [[0, 0, 0, 0], [0, 0, 0, 0]]
                pts_bank_evac = [0, 0]
                pts_war_pe = 0
                av_evac = 0
                ot_dma = 0
                last_s_mm = 0
                stt_done = {-2: 0, -1: 0}
                pnorm_done = {-2: 0, -1: 0}
                tr_pe_par = [0, 0]
                vg_dma = [0, 0]
                vg_pe = [0, 0]
                qt_setup_done = {}
                evac_ms = {}
                GL = [(qt, hg) for qt in range(8) for hg in range(8)]

                def stageA(Gi):
                    global_vars = None
                    qt, hg = GL[Gi]
                    s = Gi % 2
                    E = E_OF_QT[qt]
                    KS = KS_OF_QT[qt]
                    KT4 = KS // 128
                    nonlocal last_s_mm, qd_ms
                    if hg == 0:
                        k.do("vector", lambda e, KS=KS, qt=qt: e.tensor_scalar(
                            t_j60[:, 0:KS], g_iota[:, 0:KS], g_cst[:, qt:qt + 1],
                            BIG, op0=OP.is_ge, op1=OP.mult))
                        k.do("vector", lambda e, KS=KS: e.tensor_scalar(
                            t_j60[:, 0:KS], t_j60[:, 0:KS], SHIFT, None,
                            op0=OP.subtract))
                        vfence()
                        k.wait("sync", "qev", qscr_done)
                        k.wait("sync", "sp", last_s_mm)
                        qd_ms = k.do("sync", lambda e, qt=qt: e.dma_start(
                            out=t_qtb.rearrange("p (h t) -> p h t", h=H),
                            in_=qt_scr[:, :, qt * 128:(qt + 1) * 128].rearrange(
                                "h p t -> p h t")), "qtb", 16)
                    vj = k.c["sv"]
                    gkv = hg
                    vsem = "hra" if Gi % 2 == 0 else "hrb"
                    k.wait("sync", "sp", vg_pe[Gi % 2])
                    k.wait("sync", "qev", qscr_done)
                    vg_dma[Gi % 2] = k.do("sync",
                                          lambda e, KT4=KT4, gkv=gkv, sl=Gi % 2:
                                          e.dma_start(
                        out=t_vg[sl][:, 0:KT4 * 128].rearrange(
                            "p (c d) -> p c d", c=KT4),
                        in_=v_scr[0:KT4, :, gkv * 128:(gkv + 1) * 128].rearrange(
                            "c p d -> p c d")), vsem, 16)
                    ev = []
                    for j in range(4):
                        h = 4 * hg + j
                        slot = (j % 2) if E == 2 else 0
                        sb = pA[:, slot * 1024: slot * 1024 + KS]
                        k.wait("tensor", "qtb", qd_ms)
                        k.wait("tensor", "sa", ktr_done)
                        k.wait("tensor", "sv", pa_free[slot])
                        k.wait("tensor", "sa", shat_act[s][j])
                        for c in range(E):
                            k.do("tensor", lambda e, slot=slot, c=c, h=h, gkv=gkv:
                                 e.matmul(
                                pA[:, slot * 1024 + c * 512:
                                   slot * 1024 + (c + 1) * 512],
                                t_qtb[:, h * 128:(h + 1) * 128],
                                a_ktr[:, gkv * S + c * 512:
                                      gkv * S + (c + 1) * 512],
                                start=True, stop=True),
                                "sp", 1 if c == E - 1 else None)
                        pe_sm = k.c["sp"]
                        last_s_mm = pe_sm
                        k.wait("vector", "sp", pe_sm)
                        k.wait("vector", "sa", shat_act[s][j])
                        pa_free[slot] = k.do(
                            "vector", lambda e, s=s, j=j, sb=sb, KS=KS:
                            e.scalar_tensor_tensor(
                                t_shat[s][j][:, 0:KS], sb, INV_SQRT_HD,
                                t_j60[:, 0:KS], op0=OP.mult, op1=OP.subtract),
                            "sv", 1)
                        ev.append(k.c["sv"])
                    evac_ms[Gi] = ev

                def stageBC(Gi):
                    nonlocal ot_dma, pts_war_pe, av_evac
                    qt, hg = GL[Gi]
                    s = Gi % 2
                    E = E_OF_QT[qt]
                    KS = KS_OF_QT[qt]
                    KT4 = KS // 128
                    gkv = hg
                    sh = t_shat[s]
                    if hg == 0:
                        k.do("vector", lambda e, qt=qt: e.tensor_copy(
                            st(ST_THR)[:, 0:2],
                            g_cst[:, 16 + qt:17 + qt].to_broadcast([128, 2])))
                        k.do("vector", lambda e, qt=qt: e.tensor_copy(
                            st(ST_THR)[:, 2:4],
                            g_cst[:, 8 + qt:9 + qt].to_broadcast([128, 2])))
                        k.do("vector", lambda e, qt=qt: e.tensor_copy(
                            st(ST_THRQ)[:, 0:2],
                            g_cst[:, 96 + qt:97 + qt].to_broadcast([128, 2])))
                        k.do("vector", lambda e, qt=qt: e.tensor_copy(
                            st(ST_THRQ)[:, 2:4],
                            g_cst[:, 88 + qt:89 + qt].to_broadcast([128, 2])))
                        k.nop("vector")
                    k.do("vector", lambda e: e.tensor_scalar(
                        st(ST_TC)[:], st(ST_HALF)[:], 0.0, T0,
                        op0=OP.mult, op1=OP.add))
                    k.do("vector", lambda e: e.tensor_scalar(
                        st(ST_NG)[:], st(ST_HALF)[:], 0.0, -T0,
                        op0=OP.mult, op1=OP.add))
                    vng = vfence()
                    ev = evac_ms[Gi]
                    del evac_ms[Gi]
                    for it in range(NSUB + NFULL):
                        subp = it < NSUB
                        w_i = W_SUB[it] if subp else W_FULL[it - NSUB]
                        thr_blk = ST_THRQ if subp else ST_THR
                        k.wait("scalar", "sv", vng)
                        k.wait("scalar", "sv", ev[1])
                        for j in (0, 1):
                            if subp:
                                k.do("scalar", lambda e, j=j, KS=KS, sh=sh:
                                     e.activation(
                                    g_junkb[:, 0:KS // 4], sh[j][:, 0:KS:4],
                                    AF.Sign, bias=st(ST_NG)[:, j:j + 1],
                                    scale=1.0,
                                    accum_out=st(ST_CNT)[:, j:j + 1]),
                                    "sa", 1 if j == 1 else None)
                            else:
                                k.do("scalar", lambda e, j=j, KS=KS, sh=sh:
                                     e.activation(
                                    g_junkb[:, 0:KS], sh[j][:, 0:KS],
                                    AF.Sign, bias=st(ST_NG)[:, j:j + 1],
                                    scale=1.0,
                                    accum_out=st(ST_CNT)[:, j:j + 1]),
                                    "sa", 1 if j == 1 else None)
                        act_c = k.c["sa"]
                        for j in (2, 3):
                            if subp:
                                k.do("vector", lambda e, j=j, KS=KS, sh=sh:
                                     e.tensor_scalar(
                                    g_junkc[:, 0:KS // 4], sh[j][:, 0:KS:4],
                                    st(ST_TC)[:, j:j + 1], None, op0=OP.is_ge,
                                    op1=OP.add,
                                    accum_out=st(ST_CNT)[:, j:j + 1]))
                            else:
                                k.do("vector", lambda e, j=j, KS=KS, sh=sh:
                                     e.tensor_scalar(
                                    g_junkc[:, 0:KS], sh[j][:, 0:KS],
                                    st(ST_TC)[:, j:j + 1], None, op0=OP.is_ge,
                                    op1=OP.add,
                                    accum_out=st(ST_CNT)[:, j:j + 1]))
                        k.wait("vector", "sa", act_c)
                        k.nop("vector")
                        vfence()
                        k.do("vector", lambda e, thr_blk=thr_blk: e.tensor_tensor(
                            st(ST_P)[:], st(ST_CNT)[:], st(thr_blk)[:],
                            op=OP.is_ge))
                        k.do("vector", lambda e, w_i=w_i: e.tensor_scalar(
                            st(ST_SU)[:], st(ST_P)[:], 2.0 * w_i, -w_i,
                            op0=OP.mult, op1=OP.add))
                        k.do("vector", lambda e: e.tensor_tensor(
                            st(ST_TC)[:], st(ST_TC)[:], st(ST_SU)[:], op=OP.add))
                        k.do("vector", lambda e: e.tensor_scalar(
                            st(ST_NG)[:], st(ST_TC)[:], -1.0, None, op0=OP.mult))
                        vng = vfence()
                    w_last = W_FULL[-1]
                    k.do("vector", lambda e, w_last=w_last: e.tensor_scalar(
                        st(ST_TS)[:], st(ST_TC)[:], 2.0 * w_last, None,
                        op0=OP.add))
                    k.do("vector", lambda e: e.tensor_scalar(
                        st(ST_NG)[:], st(ST_TS)[:], -1.0, None, op0=OP.mult))
                    vng = vfence()
                    k.wait("scalar", "sv", vng)
                    for j in (0, 1):
                        k.do("scalar", lambda e, j=j, KS=KS, sh=sh: e.activation(
                            g_junkb[:, 0:KS], sh[j][:, 0:KS], AF.Sign,
                            bias=st(ST_NG)[:, j:j + 1], scale=1.0,
                            accum_out=st(ST_CF)[:, j:j + 1]))
                    act_cf = afence()
                    for j in (2, 3):
                        k.do("vector", lambda e, j=j, KS=KS, sh=sh: e.tensor_scalar(
                            g_junkc[:, 0:KS], sh[j][:, 0:KS],
                            st(ST_TS)[:, j:j + 1], None, op0=OP.is_ge,
                            op1=OP.add, accum_out=st(ST_CF)[:, j:j + 1]))
                    k.wait("vector", "sa", act_cf)
                    k.nop("vector")
                    vfence()
                    k.do("vector", lambda e: e.tensor_tensor(
                        st(ST_D)[:], st(ST_THR)[:], st(ST_CF)[:], op=OP.subtract))
                    k.do("vector", lambda e: e.tensor_tensor(
                        st(ST_D)[:], st(ST_D)[:], st(ST_HALF)[:], op=OP.mult))
                    for j in range(4):
                        k.do("vector", lambda e, j=j, KS=KS, sh=sh:
                             e.scalar_tensor_tensor(
                            t_m[:, 0:KS], sh[j][:, 0:KS],
                            st(ST_TS)[:, j:j + 1], sh[j][:, 0:KS],
                            op0=OP.is_lt, op1=OP.mult))
                        k.do("vector", lambda e, j=j, KS=KS: e.max(
                            out=st(ST_MV)[:, j * 8:(j + 1) * 8],
                            in_=t_m[:, 0:KS]))
                    for j in range(4):
                        k.do("vector", lambda e, j=j: e.tensor_copy(
                            st(ST_DREP)[:, j * 8:(j + 1) * 8],
                            st(ST_D)[:, j:j + 1].to_broadcast([128, 8])))
                    k.nop("vector")
                    k.do("vector", lambda e: e.tensor_tensor(
                        st(ST_P)[:], g_cst[:, 24:88], st(ST_DREP)[:],
                        op=OP.is_gt))
                    k.do("vector", lambda e: e.tensor_scalar(
                        st(ST_SU)[:], st(ST_P)[:], BIG, None, op0=OP.mult))
                    k.do("vector", lambda e: e.tensor_tensor(
                        st(ST_MV)[:], st(ST_MV)[:], st(ST_SU)[:], op=OP.add))
                    k.do("vector", lambda e: e.tensor_reduce(
                        st(ST_T4)[:, 0:8],
                        st(ST_MV)[:].rearrange("p (t e) -> p t e", e=8),
                        axis=mybir.AxisListType.X, op=OP.min))
                    k.nop("vector")
                    k.do("vector", lambda e: e.tensor_tensor(
                        st(ST_T4)[:, 0:4], st(ST_T4)[:, 0:4],
                        st(ST_TS)[:, 0:4], op=OP.min))
                    k.nop("vector")
                    k.do("vector", lambda e: e.tensor_copy(
                        st(ST_TS)[:, 0:4], st(ST_T4)[:, 0:4]))
                    k.nop("vector")
                    k.do("vector", lambda e: e.tensor_scalar(
                        st(ST_NTS)[:], st(ST_TS)[:], -1.0, None, op0=OP.mult))
                    vwalk = vfence()
                    vsem = "hra" if Gi % 2 == 0 else "hrb"
                    for j in range(4):
                        h = 4 * hg + j
                        eb = t_eb[j % 2]
                        pb = t_pb[j % 2]
                        k.wait("scalar", "sv", vwalk)
                        k.wait("scalar", "sv", stt_done[j - 2])
                        k.wait("scalar", "sp", tr_pe_par[j % 2])
                        a_x = k.do("scalar", lambda e, j=j, eb=eb, KS=KS, sh=sh:
                                   e.activation(
                            eb[:, 0:KS], sh[j][:, 0:KS], AF.Exp,
                            bias=st(ST_NTS)[:, j:j + 1], scale=1.0), "sa", 1)
                        shat_act[s][j] = a_x
                        k.wait("vector", "sa", a_x)
                        k.wait("vector", "sa", pnorm_done[j - 2])
                        v_p = k.do("vector",
                                   lambda e, j=j, eb=eb, pb=pb, KS=KS, sh=sh:
                                   e.scalar_tensor_tensor(
                            pb[:, 0:KS], sh[j][:, 0:KS],
                            st(ST_TS)[:, j:j + 1], eb[:, 0:KS],
                            op0=OP.is_ge, op1=OP.mult,
                            accum_out=st(ST_SUMP)[:, j:j + 1]), "sv", 1)
                        stt_done[j] = v_p
                        k.nop("vector")
                        vfence()
                        k.do("vector", lambda e, j=j: e.reciprocal(
                            st(ST_RCP)[:, j:j + 1], st(ST_SUMP)[:, j:j + 1]))
                        v_r = vfence()
                        k.wait("scalar", "sv", v_r)
                        a_n = k.do("scalar",
                                   lambda e, j=j, eb=eb, pb=pb, KS=KS:
                                   e.activation(
                            eb[:, 0:KS], pb[:, 0:KS], AF.Identity,
                            bias=0.0, scale=st(ST_RCP)[:, j:j + 1]), "sa", 1)
                        pnorm_done[j] = a_n
                        k.wait("tensor", "sa", a_n)
                        ngrp = (KT4 + 3) // 4
                        for cg in range(ngrp):
                            nc4 = min(4, KT4 - cg * 4)
                            bank = pB[:, (cg % 2) * 512:(cg % 2 + 1) * 512]
                            k.wait("tensor", "sa", pts_bank_evac[cg % 2])
                            for c4 in range(nc4):
                                c = cg * 4 + c4
                                k.do("tensor",
                                     lambda e, bank=bank, c4=c4, c=c, eb=eb:
                                     e.matmul(bank[:, c4 * 128:(c4 + 1) * 128],
                                              eb[:, c * 128:(c + 1) * 128],
                                              g_ident, start=True, stop=True),
                                     "sp", 1 if c4 == nc4 - 1 else None)
                            pe_ptg = k.c["sp"]
                            k.wait("scalar", "sp", pe_ptg)
                            k.wait("scalar", "sp", pts_war_pe)
                            pts_bank_evac[cg % 2] = k.do(
                                "scalar", lambda e, bank=bank, cg=cg, nc4=nc4:
                                e.activation(
                                    t_pts[:, cg * 512:cg * 512 + nc4 * 128],
                                    bank[:, 0:nc4 * 128],
                                    AF.Identity, bias=0.0, scale=1.0), "sa", 1)
                        pts_done = k.c["sa"]
                        k.wait("tensor", "sa", pts_done)
                        k.wait("tensor", "sa", av_evac)
                        k.wait("tensor", vsem, vg_dma[Gi % 2])
                        for c in range(KT4):
                            k.do("tensor", lambda e, c=c, KT4=KT4, sl=Gi % 2:
                                 e.matmul(
                                pB[:, 1536:1664],
                                t_vg[sl][:, c * 128:(c + 1) * 128],
                                t_pts[:, c * 128:(c + 1) * 128],
                                start=(c == 0), stop=(c == KT4 - 1)),
                                "sp", 1 if c == KT4 - 1 else None)
                        av_d = k.c["sp"]
                        pts_war_pe = av_d
                        tr_pe_par[j % 2] = av_d
                        vg_pe[Gi % 2] = av_d
                        k.wait("scalar", "sp", av_d)
                        k.wait("scalar", "otd", ot_dma)
                        av_evac = k.do("scalar", lambda e, h=h: e.activation(
                            t_ot[:, h * 128:(h + 1) * 128], pB[:, 1536:1664],
                            AF.Identity, bias=0.0, scale=1.0), "sa", 1)
                    if hg == 7:
                        k.wait("sync", "sa", k.c["sa"])
                        ot_dma = k.do("sync", lambda e, qt=qt: e.dma_start(
                            out=ot_scr[qt, :, :], in_=t_ot[:]), "otd", 16)
                        if "dbg_shat" in dbg and qt == 7:
                            k.barrier()
                            for j in range(4):
                                k.do("sync", lambda e, j=j, sh=sh: e.dma_start(
                                    out=dbg["dbg_shat"][:, j * 2048:(j + 1) * 2048],
                                    in_=sh[j]), "dbgw", 16)
                            k.do("sync", lambda e: e.dma_start(
                                out=dbg["dbg_state"], in_=g_st[:]), "dbgw", 16)

                qd_ms = 0
                stageA(0)
                for Gi in range(1, len(GL)):
                    stageBC(Gi - 1)
                    stageA(Gi)
                stageBC(len(GL) - 1)
                if "dbg_ot" in dbg:
                    k.barrier()
                    for qt in range(8):
                        k.do("sync", lambda e, qt=qt: e.dma_start(
                            out=dbg["dbg_ot"][qt], in_=ot_scr[qt, :, :]), "dbgw", 16)
            k.barrier()

            if "wo" in PHASES and "r1" in PHASES:
                cw = k.do("gpsimd", lambda e: e.dma_start(out=wot_bf, in_=wot),
                          "wua", 16)
                wo_pe = [0, 0]
                bank_evac = [0, 0]
                attn_dve = 0
                otb_pe = 0
                r1_pe = 0
                r1_evac = [0, 0]
                x2dma = 0
                x2tdma = 0
                prev_stats = 0
                xq_war = 0
                for qt in range(8):
                    k.wait("sync", "otd", k.c["otd"])
                    k.wait("sync", "sp", otb_pe)
                    od = k.do("sync", lambda e, qt=qt: e.dma_start(
                        out=w_otb, in_=ot_scr[qt, :, :]), "otba", 16)
                    k.wait("sync", "sv", xq_war)
                    db = k.do("sync", lambda e, qt=qt: e.dma_start(
                        out=g_xin[:, D:2 * D],
                        in_=xq[qt * 128:(qt + 1) * 128, :]), "xa", 16)
                    for oc in range(8):
                        wb = w_wot[oc % 2]
                        k.wait("gpsimd", "sp", wo_pe[oc % 2])
                        k.wait("gpsimd", "wua", cw)
                        wd = k.do("gpsimd", lambda e, oc=oc, wb=wb: e.dma_start(
                            out=wb.rearrange("p (c o) -> p c o", c=KT),
                            in_=wot_bf[:, oc * 512:(oc + 1) * 512].rearrange(
                                "(c p) o -> p c o", p=128)),
                            "wva" if oc % 2 == 0 else "wvb", 16)
                        bank = pA[:, (oc % 2) * 512:(oc % 2 + 1) * 512]
                        k.wait("tensor", "wva" if oc % 2 == 0 else "wvb", wd)
                        k.wait("tensor", "otba", od)
                        k.wait("tensor", "sa", bank_evac[oc % 2])
                        for c in range(KT):
                            k.do("tensor", lambda e, bank=bank, c=c, wb=wb:
                                 e.matmul(bank, w_otb[:, c * 128:(c + 1) * 128],
                                          wb[:, c * 512:(c + 1) * 512],
                                          start=(c == 0), stop=(c == KT - 1)),
                                 "sp", 1 if c == KT - 1 else None)
                        mm_d = k.c["sp"]
                        wo_pe[oc % 2] = mm_d
                        if oc == 7:
                            otb_pe = mm_d
                        k.wait("scalar", "sp", mm_d)
                        k.wait("scalar", "sv", attn_dve)
                        bank_evac[oc % 2] = k.do(
                            "scalar", lambda e, bank=bank, oc=oc: e.activation(
                                w_attn[:, oc * 512:(oc + 1) * 512], bank,
                                AF.Identity, bias=0.0, scale=1.0), "sa", 1)
                    tt = qt
                    a_r = ln_stats(w_attn, "sa", k.c["sa"], pe_war=r1_pe)
                    k.wait("vector", "sa", a_r)
                    k.do("vector", lambda e: e.tensor_scalar(
                        g_scr1, w_attn, g_stats[:, 2:3], g_stats[:, 7:8],
                        op0=OP.subtract, op1=OP.mult))
                    attn_dve = k.c["sv"] + 1
                    k.wait("vector", "xa", db)
                    k.wait("vector", "sa", prev_stats)
                    k.wait("vector", "x2w", x2dma)
                    v_x2 = k.do("vector", lambda e: e.tensor_tensor(
                        g_scr2, g_scr1, g_xin[:, D:2 * D], op=OP.add), "sv", 1)
                    xq_war = v_x2
                    k.wait("sync", "sv", v_x2)
                    x2dma = k.do("sync", lambda e, tt=tt: e.dma_start(
                        out=x2_scr[tt * 128:(tt + 1) * 128, :], in_=g_scr2),
                        "x2w", 16)
                    a_r2 = ln_stats(g_scr2, "sv", v_x2, pe_war=r1_pe)
                    prev_stats = a_r2
                    k.wait("vector", "sa", a_r2)
                    k.wait("vector", "sp", r1_pe)
                    v_n2 = k.do("vector", lambda e: e.tensor_scalar(
                        g_lnout[:], g_scr2, g_stats[:, 2:3], g_stats[:, 7:8],
                        op0=OP.subtract, op1=OP.mult), "sv", 1)
                    for e8 in range(8):
                        bank = pB[:, 0:512] if e8 % 2 == 0 else pB[:, 512:1024]
                        k.wait("tensor", "sv", v_n2)
                        k.wait("tensor", "sa", r1_evac[e8 % 2])
                        for c4 in range(4):
                            c = e8 * 4 + c4
                            k.do("tensor", lambda e, bank=bank, c4=c4, c=c:
                                 e.matmul(bank[:, c4 * 128:(c4 + 1) * 128],
                                          g_lnout[:, c * 128:(c + 1) * 128],
                                          g_ident, start=True, stop=True),
                                 "sp", 1 if c4 == 3 else None)
                        r1_pe = k.c["sp"]
                        k.wait("scalar", "sp", r1_pe)
                        k.wait("scalar", "x2tw", x2tdma)
                        a_e = k.do("scalar", lambda e, bank=bank: e.activation(
                            r_x2tb[:], bank, AF.Identity, bias=0.0, scale=1.0),
                            "sa", 1)
                        r1_evac[e8 % 2] = a_e
                        k.wait("sync", "sa", a_e)
                        for c4 in range(4):
                            c = e8 * 4 + c4
                            x2tdma = k.do("sync",
                                          lambda e, c=c, c4=c4, tt=tt: e.dma_start(
                                out=x2t_scr[c, :, tt * 128:(tt + 1) * 128],
                                in_=r_x2tb[:, c4 * 128:(c4 + 1) * 128]), "x2tw", 16)
                if "dbg_x2" in dbg:
                    k.barrier()
                    k.do("sync", lambda e: e.dma_start(
                        out=dbg["dbg_x2"], in_=x2_scr), "dbgw", 16)
                if "dbg_x2t" in dbg:
                    k.barrier()
                    for c in range(KT):
                        k.do("sync", lambda e, c=c: e.dma_start(
                            out=dbg["dbg_x2t"][c], in_=x2t_scr[c, :, :]), "dbgw", 16)

            k.barrier()

            if "b1" in PHASES:
                NFC = FF // 256
                wg_pe = [0, 0]
                g_bank_act = [0, 0]
                u_bank_v = [0, 0]
                gs_v = [0, 0]
                hb_pe = [0, 0]
                htb_dma = [0, 0]
                htr_evac = [0, 0]
                dx = k.do("sync", lambda e: e.dma_start(
                    out=b1_x2t.rearrange("p (c t) -> p c t", c=KT),
                    in_=x2t_scr.rearrange("c p t -> p c t")), "xa", 16)
                k.wait("tensor", "xa", dx)

                def b1_tail(fc, tt):
                    par = tt % 2
                    gbank = pA[:, 0:256] if par == 0 else pA[:, 512:768]
                    ubank = pA[:, 1024:1280] if par == 0 else pA[:, 1536:1792]
                    gs = b1_gs[par]
                    hb = b1_hb[par]
                    k.wait("scalar", "sp", b1_gd[(fc, tt)])
                    k.wait("scalar", "sv", gs_v[par])
                    a_s = k.do("scalar", lambda e, gbank=gbank, gs=gs: e.activation(
                        gs, gbank, AF.Silu, bias=0.0, scale=1.0), "sa", 1)
                    g_bank_act[par] = a_s
                    k.wait("vector", "sa", a_s)
                    k.wait("vector", "sp", b1_ud[(fc, tt)])
                    k.wait("vector", "sp", hb_pe[par])
                    v_h = k.do("vector", lambda e, ubank=ubank, gs=gs, hb=hb:
                               e.tensor_tensor(hb, gs, ubank, op=OP.mult), "sv", 1)
                    gs_v[par] = v_h
                    u_bank_v[par] = v_h
                    b1_vh[(fc, tt)] = v_h

                def b1_tr(fc, tt):
                    par = tt % 2
                    hb = b1_hb[par]
                    htb = b1_htb[par]
                    bank = pB[:, 0:256] if par == 0 else pB[:, 512:768]
                    k.wait("tensor", "sv", b1_vh[(fc, tt)])
                    k.wait("tensor", "sa", htr_evac[par])
                    for c2 in range(2):
                        k.do("tensor", lambda e, c2=c2, hb=hb, bank=bank: e.matmul(
                            bank[:, c2 * 128:(c2 + 1) * 128],
                            hb[:, c2 * 128:(c2 + 1) * 128],
                            g_ident, start=True, stop=True),
                            "sp", 1 if c2 == 1 else None)
                    hb_pe[par] = k.c["sp"]
                    k.wait("scalar", "sp", hb_pe[par])
                    k.wait("scalar", "htw", htb_dma[par])
                    htr_evac[par] = k.do("scalar", lambda e, bank=bank, htb=htb:
                                         e.activation(
                        htb, bank, AF.Identity, bias=0.0, scale=1.0), "sa", 1)
                    k.wait("sync", "sa", htr_evac[par])
                    for c2 in range(2):
                        htb_dma[par] = k.do("sync",
                                            lambda e, fc=fc, c2=c2, tt=tt, htb=htb:
                                            e.dma_start(
                            out=ht_scr[fc * 2 + c2, :, tt * 128:(tt + 1) * 128],
                            in_=htb[:, c2 * 128:(c2 + 1) * 128]), "htw", 16)

                b1_gd = {}
                b1_ud = {}
                b1_vh = {}
                prev = None
                for fc in range(NFC):
                    wgb = b1_wg[fc % 2]
                    wub = b1_wu[fc % 2]
                    k.wait("gpsimd", "sp", wg_pe[fc % 2])
                    dg = k.do("gpsimd", lambda e, fc=fc, wgb=wgb: e.dma_start(
                        out=wgb.rearrange("p (c o) -> p c o", c=KT),
                        in_=wgt[:, fc * 256:(fc + 1) * 256].rearrange(
                            "(c p) o -> p c o", p=128)),
                        "wva" if fc % 2 == 0 else "wvb", 16)
                    du = k.do("gpsimd", lambda e, fc=fc, wub=wub: e.dma_start(
                        out=wub.rearrange("p (c o) -> p c o", c=KT),
                        in_=wut[:, fc * 256:(fc + 1) * 256].rearrange(
                            "(c p) o -> p c o", p=128)),
                        "wua" if fc % 2 == 0 else "wub", 16)
                    k.wait("tensor", "wva" if fc % 2 == 0 else "wvb", dg)
                    k.wait("tensor", "wua" if fc % 2 == 0 else "wub", du)
                    for tt in range(8):
                        par = tt % 2
                        gbank = pA[:, 0:256] if par == 0 else pA[:, 512:768]
                        ubank = pA[:, 1024:1280] if par == 0 else pA[:, 1536:1792]
                        k.wait("tensor", "sa", g_bank_act[par])
                        k.wait("tensor", "sv", u_bank_v[par])
                        for c in range(KT):
                            k.do("tensor", lambda e, gbank=gbank, c=c, tt=tt, wgb=wgb:
                                 e.matmul(gbank,
                                          b1_x2t[:, c * NQ + tt * 128:
                                                 c * NQ + (tt + 1) * 128],
                                          wgb[:, c * 256:(c + 1) * 256],
                                          start=(c == 0), stop=(c == KT - 1)),
                                 "sp", 1 if c == KT - 1 else None)
                        b1_gd[(fc, tt)] = k.c["sp"]
                        for c in range(KT):
                            k.do("tensor", lambda e, ubank=ubank, c=c, tt=tt, wub=wub:
                                 e.matmul(ubank,
                                          b1_x2t[:, c * NQ + tt * 128:
                                                 c * NQ + (tt + 1) * 128],
                                          wub[:, c * 256:(c + 1) * 256],
                                          start=(c == 0), stop=(c == KT - 1)),
                                 "sp", 1 if c == KT - 1 else None)
                        b1_ud[(fc, tt)] = k.c["sp"]
                        if tt == 7:
                            wg_pe[fc % 2] = k.c["sp"]
                        b1_tail(fc, tt)
                        if prev is not None:
                            b1_tr(*prev)
                        prev = (fc, tt)
                b1_tr(*prev)
                if "dbg_ht" in dbg:
                    k.barrier()
                    for c in range(FF // 128):
                        k.do("sync", lambda e, c=c: e.dma_start(
                            out=dbg["dbg_ht"][c], in_=ht_scr[c, :, :]), "dbgw", 16)

            k.barrier()

            if "b2" in PHASES:
                NF128 = FF // 128
                FCG = [(i * 8, min(8, NF128 - i * 8)) for i in range((NF128 + 7) // 8)]
                wd_pe = [0, 0]
                ht_pe = [0, 0]
                ev_dma = 0
                bank_act = 0
                for oc in range(8):
                    for gi, (f0, nf) in enumerate(FCG):
                        wdb = b2_wd[gi % 2]
                        htb = b2_ht[gi % 2]
                        k.wait("gpsimd", "sp", wd_pe[gi % 2])
                        dwd = k.do("gpsimd",
                                   lambda e, f0=f0, nf=nf, oc=oc, wdb=wdb:
                                   e.dma_start(
                            out=wdb.rearrange("p (c o) -> p c o", c=8)[:, 0:nf, :],
                            in_=wdt[f0 * 128:(f0 + nf) * 128,
                                    oc * 512:(oc + 1) * 512].rearrange(
                                "(c p) o -> p c o", p=128)),
                            "wva" if gi % 2 == 0 else "wvb", 16)
                        k.wait("sync", "sp", ht_pe[gi % 2])
                        dht = k.do("sync", lambda e, f0=f0, nf=nf, htb=htb:
                                   e.dma_start(
                            out=htb.rearrange("p (c t) -> p c t", c=8)[:, 0:nf, :],
                            in_=ht_scr[f0:f0 + nf, :, :].rearrange(
                                "c p t -> p c t")),
                            "hra" if gi % 2 == 0 else "hrb", 16)
                        k.wait("tensor", "wva" if gi % 2 == 0 else "wvb", dwd)
                        k.wait("tensor", "hra" if gi % 2 == 0 else "hrb", dht)
                        if gi == 0:
                            k.wait("tensor", "sa", bank_act)
                        for ci in range(nf):
                            for tt in range(8):
                                bank = (pA if tt < 4 else pB)[
                                    :, (tt % 4) * 512:(tt % 4 + 1) * 512]
                                last = (gi == len(FCG) - 1 and ci == nf - 1)
                                k.do("tensor",
                                     lambda e, bank=bank, ci=ci, tt=tt, htb=htb,
                                     wdb=wdb, gi=gi, last=last:
                                     e.matmul(
                                         bank,
                                         htb[:, ci * NQ + tt * 128:
                                             ci * NQ + (tt + 1) * 128],
                                         wdb[:, ci * 512:(ci + 1) * 512],
                                         start=(gi == 0 and ci == 0), stop=last),
                                     "sp", 1 if (tt == 7 and ci == nf - 1) else None)
                        mm_d = k.c["sp"]
                        wd_pe[gi % 2] = mm_d
                        ht_pe[gi % 2] = mm_d
                    all_d = k.c["sp"]
                    for tt in range(8):
                        bank = (pA if tt < 4 else pB)[
                            :, (tt % 4) * 512:(tt % 4 + 1) * 512]
                        k.wait("scalar", "sp", all_d)
                        k.wait("scalar", "yw", ev_dma)
                        a_e = k.do("scalar", lambda e, bank=bank: e.activation(
                            b2_yev[:, 0:512], bank, AF.Identity, bias=0.0,
                            scale=1.0), "sa", 1)
                        k.wait("sync", "sa", a_e)
                        ev_dma = k.do("sync", lambda e, tt=tt, oc=oc: e.dma_start(
                            out=y_scr[tt * 128:(tt + 1) * 128,
                                      oc * 512:(oc + 1) * 512],
                            in_=b2_yev[:, 0:512]), "yw", 16)
                    bank_act = k.c["sa"]
                if "dbg_y" in dbg:
                    k.barrier()
                    k.do("sync", lambda e: e.dma_start(
                        out=dbg["dbg_y"], in_=y_scr), "dbgw", 16)

            k.barrier()

            if "r2" in PHASES:
                out_dma = 0
                buf_war = 0
                prev_stats = 0
                for tt in range(8):
                    xa = g_xin[:, 0:D]
                    xb = g_xin[:, D:2 * D]
                    k.wait("sync", "sv", buf_war)
                    da = k.do("sync", lambda e, tt=tt: e.dma_start(
                        out=xa, in_=y_scr[tt * 128:(tt + 1) * 128, :]), "xa", 16)
                    db = k.do("sync", lambda e, tt=tt: e.dma_start(
                        out=xb, in_=x2_scr[tt * 128:(tt + 1) * 128, :]), "xa", 16)
                    a_r = ln_stats(xa, "xa", db)
                    k.wait("vector", "sa", a_r)
                    k.wait("vector", "outw", out_dma)
                    k.do("vector", lambda e: e.tensor_scalar(
                        g_scr1, xa, g_stats[:, 2:3], g_stats[:, 7:8],
                        op0=OP.subtract, op1=OP.mult))
                    k.wait("vector", "xa", db)
                    buf_war = k.do("vector", lambda e: e.tensor_tensor(
                        g_scr2, g_scr1, xb, op=OP.add), "sv", 1)
                    k.wait("sync", "sv", buf_war)
                    out_dma = k.do("sync", lambda e, tt=tt: e.dma_start(
                        out=out_d[tt * 128:(tt + 1) * 128, :], in_=g_scr2),
                        "outw", 16)
                k.wait("sync", "outw", k.c["outw"])

        kb = KB(blk, sems)
        kb.run(script)
    return nc


def _host_tables():
    t = np.arange(S, dtype=np.float32)
    inv = (1.0 / (10000.0 ** (np.arange(0, HD, 2).astype(np.float32) / HD))).astype(np.float32)
    fr = np.outer(t, inv).astype(np.float32)
    emb = np.concatenate([fr, fr], axis=1)
    cos = np.cos(emb).astype(np.float32)
    sin = np.sin(emb).astype(np.float32)
    sinsig = np.concatenate([-sin[:, :64], sin[:, 64:]], axis=1).astype(np.float32)
    return cos, sinsig


def _core_meta(core):
    b = core // 2
    blocks = (0, 3) if core % 2 == 0 else (1, 2)
    qpos = np.concatenate([np.arange(blocks[0] * 512, blocks[0] * 512 + 512),
                           np.arange(blocks[1] * 512, blocks[1] * 512 + 512)])
    return b, blocks, qpos


def _in_maps(hs, wqt, wkt, wvt, wot, wgt, wut, wdt):
    cos, sinsig = _host_tables()
    cosk = np.ascontiguousarray(cos.reshape(16, 128, HD).transpose(1, 0, 2))
    sink = np.ascontiguousarray(sinsig.reshape(16, 128, HD).transpose(1, 0, 2))
    iota = np.broadcast_to(np.arange(S, dtype=np.uint16), (128, S)).copy()
    ident = np.eye(128, dtype=np.float32).astype(ml_dtypes.bfloat16)
    maps = []
    for core in range(8):
        b, blocks, qpos = _core_meta(core)
        xq = np.ascontiguousarray(hs[b][qpos])
        xkv = np.ascontiguousarray(hs[b])
        cq = np.ascontiguousarray(cos[qpos].reshape(8, 128, HD).transpose(1, 0, 2))
        sq = np.ascontiguousarray(sinsig[qpos].reshape(8, 128, HD).transpose(1, 0, 2))
        me = (qpos + 1).astype(np.float32)
        targ = np.minimum(MASK_K, qpos + 1).astype(np.float32)
        ks_row = np.repeat(np.array(KS_OF_QT, np.float32), 128)
        t2 = 2.0 * targ - ks_row
        cst = np.zeros((128, 104), np.float32)
        cst[:, 0:8] = me.reshape(8, 128).T
        cst[:, 8:16] = targ.reshape(8, 128).T
        cst[:, 16:24] = t2.reshape(8, 128).T
        cst[:, 24:88] = np.broadcast_to(
            np.tile(np.arange(1, 9, dtype=np.float32), 8), (128, 64))
        cst[:, 88:96] = (targ / 4.0).reshape(8, 128).T
        cst[:, 96:104] = (t2 / 4.0).reshape(8, 128).T
        maps.append(dict(
            xq=xq, xkv=xkv, wqt=wqt, wkt=wkt, wvt=wvt, wot=wot,
            wgt=wgt, wut=wut, wdt=wdt,
            cosq=cq, sinq=sq, cosk=cosk, sink=sink,
            consts=cst, iota_in=iota, ident_in=ident))
    return maps


def kernel(L0_attn_hidden_states, L0_attn_Wq, L0_attn_Wk, L0_attn_Wv, L0_attn_Wo,
           L0_ffn_W_gate, L0_ffn_W_up, L0_ffn_W_down):
    hs = np.ascontiguousarray(L0_attn_hidden_states, dtype=np.float32)
    maps = _in_maps(
        hs,
        np.ascontiguousarray(L0_attn_Wq.T, dtype=np.float32),
        np.ascontiguousarray(L0_attn_Wk.T, dtype=np.float32),
        np.ascontiguousarray(L0_attn_Wv.T, dtype=np.float32),
        np.ascontiguousarray(L0_attn_Wo.T, dtype=np.float32),
        np.ascontiguousarray(L0_ffn_W_gate.T, dtype=np.float32),
        np.ascontiguousarray(L0_ffn_W_up.T, dtype=np.float32),
        np.ascontiguousarray(L0_ffn_W_down.T, dtype=np.float32))
    nc = bass.Bass("TRN2", target_bir_lowering=False)
    build(nc)
    res = run_bass_kernel_spmd(nc, maps, core_ids=list(range(8)))
    out = np.empty((NB, S, D), np.float32)
    for core in range(8):
        b, blocks, _ = _core_meta(core)
        o = res.results[core]["out"]
        out[b, blocks[0] * 512:blocks[0] * 512 + 512] = o[0:512]
        out[b, blocks[1] * 512:blocks[1] * 512 + 512] = o[512:1024]
    return out
